# version 06 (final)
# baseline (speedup 1.0000x reference)
import sys
if '/opt/trn_rl_repo' not in sys.path:
    sys.path.insert(0, '/opt/trn_rl_repo')
"""Megakernel builder library for mk_macaron_d4 (feature-major, batch-sharded).

Conventions:
  - fm (feature-major): [d rows -> 128-partition tiles, T cols].
  - Residual stream X: per-core DRAM fm fp32 [1024, T].
  - Weights host-pre-transposed to [d_in, d_out] fp32 DRAM; cast to bf16 on load.
  - matmul: out[m, n] = lhsT[k, m].T @ rhs[k, n].
  - Engine sections are emitted with @sec(b, 'engine'); sections NEVER nest.
"""
import numpy as np
import concourse.bass as bass
import concourse.mybir as mybir
from contextlib import ExitStack

F32 = mybir.dt.float32
BF16 = mybir.dt.bfloat16
AF = mybir.ActivationFunctionType
ALU = mybir.AluOpType
AX = mybir.AxisListType

P = 128
NMAX = 512


class Sem:
    def __init__(self, handle, name):
        self.h = handle
        self.name = name
        self.count = 0

    def inc(self, instr, n=1):
        instr.then_inc(self.h, n)
        self.count += n
        return instr


class B:
    def __init__(self, name="mk"):
        self.nc = bass.Bass("TRN2", target_bir_lowering=False,
                            detect_race_conditions=False)
        self.es = ExitStack()
        self._sems = {}
        self.block = None
        self.n_sections = 0
        self._uid = 0

    def uid(self):
        self._uid += 1
        return self._uid

    def sem(self, name):
        if name not in self._sems:
            h = self.es.enter_context(self.nc.semaphore(name))
            self._sems[name] = Sem(h, name)
        return self._sems[name]

    def fresh_qsem(self, prefix="qw"):
        self._uid += 1
        return self.sem(f"{prefix}_{self._uid}")

    def sbuf(self, name, shape, dtype):
        return self.es.enter_context(self.nc.sbuf_tensor(name, list(shape), dtype))

    def psum(self, name, shape, dtype=F32):
        return self.es.enter_context(self.nc.psum_tensor(name, list(shape), dtype))

    def dram(self, name, shape, dtype, kind="Internal"):
        return self.nc.dram_tensor(name, list(shape), dtype, kind=kind).ap()

    def open_block(self):
        self.block = self.es.enter_context(self.nc.Block())

    def close(self):
        self.es.close()

    def n_inst(self):
        return sum(1 for _ in self.nc.all_instructions())


class PlannedInstr:
    __slots__ = ("incs",)

    def __init__(self):
        self.incs = []

    def then_inc(self, sem_h, n):
        self.incs.append((sem_h, n))
        return self


class EngProxy:

    def __init__(self, b, which):
        self._b = b
        self._which = which
        self._ops = []

    def wait_ge(self, sem_h, value):
        self._ops.append(("wait_ge", (sem_h, int(value)), {}, None))

    def __getattr__(self, name):
        if name.startswith("_"):
            raise AttributeError(name)

        def rec(*args, **kwargs):
            pi = PlannedInstr()
            self._ops.append((name, args, kwargs, pi))
            return pi
        return rec

    def _replay(self, e):
        for name, args, kwargs, pi in self._ops:
            if name == "wait_ge":
                e.wait_ge(*args)
                continue
            ins = getattr(e, name)(*args, **kwargs)
            if pi is not None:
                for h, n in pi.incs:
                    ins.then_inc(h, n)


def sec(b, which):
    b.n_sections += 1
    if not hasattr(b, "proxies"):
        b.proxies = {w: EngProxy(b, w) for w in
                     ("sync", "vector", "scalar", "tensor", "gpsimd")}

    def deco(fn):
        fn(b.proxies[which])
    return deco


def finish(b):
    for w in ("sync", "vector", "scalar", "tensor", "gpsimd"):
        if not hasattr(b, "proxies"):
            break
        px = b.proxies[w]
        if not px._ops:
            continue
        getattr(b.block, w)(px._replay)


class Common:
    CONST_VALUES = (0.0, 1.0, -1.0, 0.5, -0.5, 2.0, 1e-5, 1e-6, 1e-8, 0.25,
                    1.0 / 1024.0, 1.0 / 8.0, 1.702, 0.044715, 40.0, -40.0,
                    0.7978845608028654, 0.70710678118654752, 1.0 / 512.0,
                    1.0 / 64.0, 1.0 / 16.0, 0.125, -4.6, 8192.0, 4096.0, 6.4e-4)

    def __init__(self, b: B):
        self.b = b
        self.ident_bf = b.sbuf("c_ident_bf", [P, P], BF16)
        self.ones_col = b.sbuf("c_ones_col", [P, 1], BF16)
        self.ones_col_f = b.sbuf("c_ones_col_f", [P, 1], F32)
        self.ones_row = b.sbuf("c_ones_row", [1, P], BF16)
        self.ones_row_f = b.sbuf("c_ones_row_f", [1, P], F32)
        self.iota_f = b.sbuf("c_iota_f", [P, P], F32)
        self.const_tile = b.sbuf("c_consts", [P, len(self.CONST_VALUES)], F32)
        self.sel16 = b.sbuf("c_sel16", [16, 16 * P], BF16)
        self.m_up_incl_f = b.sbuf("c_mui", [P, P], F32)
        self.m_up_strict_f = b.sbuf("c_mus", [P, P], F32)
        self.m_lo_strict_f = b.sbuf("c_mls", [P, P], F32)
        self.m_us_bf = b.sbuf("c_musb", [P, P], BF16)
        self.m_ls_bf = b.sbuf("c_mlsb", [P, P], BF16)
        self.ident_f = b.sbuf("c_identf", [P, P], F32)
        self.gap_scr = b.sbuf("c_gapscr", [P, 64], F32)
        self.cidx = {float(v): i for i, v in enumerate(self.CONST_VALUES)}
        self.ready = b.sem("c_ready")

    def gap(self, v):
        return v.tensor_copy(out=self.gap_scr[:, :], in_=self.iota_f[:, 0:64])

    def const_pp(self, value):
        v = float(value)
        assert v in self.cidx, f"const {v} not pre-registered"
        return self.const_tile[:, self.cidx[v]:self.cidx[v] + 1]

    def emit(self):
        b = self.b
        gg = b.sem("c_gg")
        com = self

        @sec(b, "gpsimd")
        def _(g):
            g.iota(com.iota_f[:, :], pattern=[[1, P]], base=0,
                   channel_multiplier=-1, allow_small_or_imprecise_dtypes=True)
            gg.inc(g.iota(com.sel16[:, :].rearrange("p (n j) -> p n j", j=P),
                          pattern=[[1, 16], [0, P]], base=0, channel_multiplier=-1,
                          allow_small_or_imprecise_dtypes=True))

        @sec(b, "vector")
        def _(v):
            v.wait_ge(gg.h, gg.count)
            v.tensor_single_scalar(out=com.ident_bf[:, :], in_=com.iota_f[:, :],
                                   scalar=0.0, op=ALU.is_equal)
            v.tensor_single_scalar(out=com.sel16[:, :], in_=com.sel16[:, :],
                                   scalar=0.0, op=ALU.is_equal)
            v.tensor_single_scalar(out=com.m_up_incl_f[:, :], in_=com.iota_f[:, :],
                                   scalar=0.0, op=ALU.is_ge)
            v.tensor_single_scalar(out=com.m_up_strict_f[:, :], in_=com.iota_f[:, :],
                                   scalar=0.0, op=ALU.is_gt)
            v.tensor_single_scalar(out=com.m_lo_strict_f[:, :], in_=com.iota_f[:, :],
                                   scalar=0.0, op=ALU.is_lt)
            v.tensor_copy(out=com.m_us_bf[:, :], in_=com.m_up_strict_f[:, :])
            v.tensor_copy(out=com.m_ls_bf[:, :], in_=com.m_lo_strict_f[:, :])
            v.tensor_single_scalar(out=com.ident_f[:, :], in_=com.iota_f[:, :],
                                   scalar=0.0, op=ALU.is_equal)
            v.memset(com.ones_col[:, :], 1.0)
            v.memset(com.ones_col_f[:, :], 1.0)
            v.memset(com.ones_row[:, :], 1.0)
            v.memset(com.ones_row_f[:, :], 1.0)
            items = list(com.cidx.items())
            for val, i in items[:-1]:
                v.memset(com.const_tile[:, i:i + 1], val)
            com.ready.inc(v.memset(com.const_tile[:, items[-1][1]:items[-1][1] + 1],
                                   items[-1][0]))

    def wait_ready(self, e):
        e.wait_ge(self.ready.h, self.ready.count)


class PsumPool:
    def __init__(self, b: B, n=4, name="pp", shape=(P, NMAX), dtype=F32):
        self.b = b
        self.arena = b.psum(name, [shape[0], n * shape[1]], dtype)
        self.banks = [self.arena[:, i * shape[1]:(i + 1) * shape[1]] for i in range(n)]
        self.n = n
        self.next = 0
        self.pending = [None] * n

    def acquire(self, t_eng):
        i = self.next % self.n
        self.next += 1
        if self.pending[i] is not None:
            sem, cnt = self.pending[i]
            t_eng.wait_ge(sem.h, cnt)
            self.pending[i] = None
        return i, self.banks[i]

    def release(self, i, sem):
        self.pending[i] = (sem, sem.count)


class WRes:

    def __init__(self, b, name, d_in, d_out):
        self.b = b
        self.d_in, self.d_out = d_in, d_out
        self.kt = (d_in + P - 1) // P
        self.tiles = [b.sbuf(f"{name}_k{k}",
                             [min(P, d_in - k * P), d_out], BF16)
                      for k in range(self.kt)]
        self.sem = None
        self.target = None

    def load(self, g, wdram, sem):
        for k in range(self.kt):
            r0 = k * P
            r1 = min(self.d_in, r0 + P)
            sem.inc(g.dma_start(out=self.tiles[k][:, :], in_=wdram[r0:r1, :]), 16)
        self.sem, self.target = sem, sem.count

    def wait(self, e):
        e.wait_ge(self.sem.h, self.target)


class FmBuf:
    def __init__(self, b, name, d, T, dtype=BF16):
        self.d, self.T, self.dtype = d, T, dtype
        self.nt = (d + P - 1) // P
        self.tiles = [b.sbuf(f"{name}_{i}", [min(P, d - i * P), T], dtype)
                      for i in range(self.nt)]

    def __getitem__(self, i):
        return self.tiles[i]


class Cols:

    def __init__(self, b, name, d):
        self.d = d
        self.nt = (d + P - 1) // P
        self.tile = b.sbuf(name, [P, self.nt], F32)

    def load(self, e, dram, sem):
        sem.inc(e.dma_start(out=self.tile[:, :], in_=dram[:, :]), 16)
        self.sem, self.target = sem, sem.count

    def pp(self, ot):
        return self.tile[:, ot:ot + 1]


class SubCtx:

    def __init__(self, b, com, T, nt=8, big_cols=16 * NMAX, nbuf=2):
        self.T = T
        self.nt = nt
        self.nbuf = nbuf
        self.ntc = T // NMAX
        self.xl = FmBuf(b, "xl", nt * P, T, BF16)
        self.xchunk = [b.sbuf(f"sc_xc{i}", [P, NMAX], F32) for i in range(nbuf * nt)]
        self.bigscr = b.sbuf("sc_big", [P, big_cols], BF16)
        self.sq = [self.bigscr[:, i * NMAX:(i + 1) * NMAX] for i in range(2 * nt)]
        self.rows = b.sbuf("sc_rows", [1, 4 * NMAX], F32)
        self.scr = b.sbuf("sc_scr", [P, NMAX], F32)
        self.pool = PsumPool(b, 4, "ppm")
        self.aux = PsumPool(b, 2, "ppa")
        self.aux2 = PsumPool(b, 2, "ppb")



def linear_stage(b, com, *, src_tiles, w, pool, t0, tc, mm_sem, done_sem,
                 pre_waits=(), o_tiles=None, src_t0=None):
    assert tc <= NMAX
    s0 = t0 if src_t0 is None else src_t0
    ot_list = list(range((w.d_out + P - 1) // P)) if o_tiles is None else o_tiles
    descs = []
    done_base = done_sem.count

    @sec(b, "tensor")
    def _(t):
        for sem, cnt in pre_waits:
            t.wait_ge(sem.h, cnt)
        for idx, ot in enumerate(ot_list):
            o0 = ot * P
            o1 = min(w.d_out, o0 + P)
            if idx >= pool.n:
                t.wait_ge(done_sem.h, done_base + idx - pool.n + 1)
                pool.next += 1
                bank_i = pool.next % pool.n - 1 if False else (pool.next - 1) % pool.n
                pool.pending[bank_i] = None
                bank = pool.banks[bank_i]
            else:
                bank_i, bank = pool.acquire(t)
            for k in range(w.kt):
                ins = t.matmul(bank[0:o1 - o0, 0:tc],
                               w.tiles[k][:, o0:o1],
                               src_tiles[k][:, s0:s0 + tc],
                               start=(k == 0), stop=(k == w.kt - 1))
            mm_sem.inc(ins)
            descs.append((ot, bank_i, bank[0:o1 - o0, 0:tc], mm_sem.count, t0))
    return descs


def run_epilogues(b, engine, descs, mm_sem, pool, done_sem, fn, extra_waits=()):

    @sec(b, engine)
    def _(e):
        for sem, cnt in extra_waits:
            e.wait_ge(sem.h, cnt)
        for ot, bank_i, ps, mark, t0 in descs:
            e.wait_ge(mm_sem.h, mark)
            ins = fn(e, ot, ps, t0)
            done_sem.inc(ins)
            pool.release(bank_i, done_sem)



def ln_fm(b, com, ctx, *, xdram, T=None, d=1024, eps=1e-5, gain=None, bias=None,
          guards=(), f32_to=None):
    T = T or ctx.T
    nt = d // P
    ntc = T // NMAX
    xl, xchunk, sq, rows, scr = ctx.xl, ctx.xchunk, ctx.sq, ctx.rows, ctx.scr
    qxc = [b.sem("q_xc0"), b.sem("q_xc1")]
    sc = b.sem("g_sc")
    ev = b.sem("g_ev")
    mm = b.sem("g_mm")
    qo = [b.sem("q_ob0"), b.sem("q_ob1")]

    ap_marks = []
    sq_marks = []
    cast_marks = []
    out_marks = []

    nbuf = ctx.nbuf
    for c in range(ntc):
        t0 = c * NMAX
        db = (c % nbuf) * nt
        st = {}

        @sec(b, "sync")
        def _(s, c=c, t0=t0, db=db):
            if c < nbuf:
                for gsem, gcnt in guards:
                    s.wait_ge(gsem.h, gcnt)
            else:
                s.wait_ge(sc.h, sq_marks[c - nbuf])
                s.wait_ge(ev.h, ap_marks[c - nbuf] if f32_to is not None else cast_marks[c - nbuf])
                if f32_to is not None:
                    s.wait_ge(qo[c % 2].h, out_marks[c - nbuf])
            for i in range(nt):
                qxc[c % nbuf].inc(s.dma_start(out=xchunk[db + i][:, :],
                                              in_=xdram[i * P:(i + 1) * P, t0:t0 + NMAX]), 16)
        dma_mark = (qxc[c % nbuf], qxc[c % nbuf].count)

        @sec(b, "scalar")
        def _(s, db=db):
            s.wait_ge(dma_mark[0].h, dma_mark[1])
            for i in range(nt):
                ins = s.activation(out=sq[db + i][:, :], in_=xchunk[db + i][:, :],
                                   func=AF.Square)
            sc.inc(ins)
        sq_marks.append(sc.count)

        @sec(b, "vector")
        def _(v, c=c, t0=t0, db=db):
            v.wait_ge(dma_mark[0].h, dma_mark[1])
            for i in range(nt):
                ins = v.tensor_copy(out=xl[i][:, t0:t0 + NMAX], in_=xchunk[db + i][:, :])
            ev.inc(ins)
        cast_marks.append(ev.count)

        @sec(b, "tensor")
        def _(t, c=c, t0=t0, db=db):
            if c == 0:
                com.wait_ready(t)
            t.wait_ge(sc.h, sq_marks[c])
            t.wait_ge(ev.h, cast_marks[c])
            st["bi0"], st["bank"] = ctx.aux.acquire(t)
            for i in range(nt):
                t.matmul(st["bank"][0:1, 0:NMAX], com.ones_col[:, :],
                         xl[i][:, t0:t0 + NMAX], start=(i == 0), stop=(i == nt - 1))
            st["bi1"], st["bank2"] = ctx.aux.acquire(t)
            for i in range(nt):
                ins = t.matmul(st["bank2"][0:1, 0:NMAX], com.ones_col[:, :],
                               sq[db + i][:, :], start=(i == 0), stop=(i == nt - 1))
            mm.inc(ins)
        stat_mark = mm.count

        R = lambda j: rows[0:1, j * NMAX:(j + 1) * NMAX]

        @sec(b, "vector")
        def _(v):
            v.wait_ge(mm.h, stat_mark)
            v.tensor_scalar_mul(out=R(0), in0=st["bank"][0:1, 0:NMAX], scalar1=1.0 / d)
            v.tensor_scalar_mul(out=R(1), in0=st["bank2"][0:1, 0:NMAX], scalar1=1.0 / d)
            v.tensor_tensor(out=R(2), in0=R(0), in1=R(0), op=ALU.mult)
            v.tensor_tensor(out=R(1), in0=R(1), in1=R(2), op=ALU.subtract)
            ev.inc(v.tensor_scalar_add(out=R(1), in0=R(1), scalar1=float(eps)))
            ctx.aux.release(st["bi0"], ev)
            ctx.aux.release(st["bi1"], ev)
        var_mark = ev.count

        @sec(b, "scalar")
        def _(s):
            s.wait_ge(ev.h, var_mark)
            s.activation(out=R(2), in_=R(1), func=AF.Ln)
            sc.inc(s.activation(out=R(2), in_=R(2), func=AF.Exp, scale=-0.5))
        rstd_mark = sc.count

        @sec(b, "vector")
        def _(v):
            v.wait_ge(sc.h, rstd_mark)
            ev.inc(v.tensor_tensor(out=R(3), in0=R(0), in1=R(2), op=ALU.mult))
        mr_mark = ev.count

        @sec(b, "tensor")
        def _(t):
            t.wait_ge(ev.h, mr_mark)
            st["bj0"], st["bank_r"] = ctx.aux2.acquire(t)
            t.matmul(st["bank_r"][:, 0:NMAX], com.ones_row_f[0:1, :], R(2),
                     start=True, stop=True)
            st["bj1"], st["bank_mr"] = ctx.aux2.acquire(t)
            mm.inc(t.matmul(st["bank_mr"][:, 0:NMAX], com.ones_row_f[0:1, :], R(3),
                            start=True, stop=True))
        bc_mark = mm.count

        @sec(b, "vector")
        def _(v, t0=t0, db=db):
            v.wait_ge(mm.h, bc_mark)
            if f32_to is None:
                for i in range(nt):
                    v.tensor_tensor(out=scr[:, 0:NMAX], in0=xl[i][:, t0:t0 + NMAX],
                                    in1=st["bank_r"][:, 0:NMAX], op=ALU.mult)
                    if gain is None:
                        ins = v.tensor_tensor(out=xl[i][:, t0:t0 + NMAX], in0=scr[:, 0:NMAX],
                                              in1=st["bank_mr"][:, 0:NMAX], op=ALU.subtract)
                    else:
                        v.tensor_tensor(out=scr[:, 0:NMAX], in0=scr[:, 0:NMAX],
                                        in1=st["bank_mr"][:, 0:NMAX], op=ALU.subtract)
                        ins = v.tensor_scalar(out=xl[i][:, t0:t0 + NMAX], in0=scr[:, 0:NMAX],
                                              scalar1=gain.pp(i), scalar2=bias.pp(i),
                                              op0=ALU.mult, op1=ALU.add)
            else:
                for i in range(nt):
                    v.tensor_tensor(out=scr[:, 0:NMAX], in0=xchunk[db + i][:, :],
                                    in1=st["bank_r"][:, 0:NMAX], op=ALU.mult)
                    v.tensor_tensor(out=xchunk[db + i][:, :], in0=scr[:, 0:NMAX],
                                    in1=st["bank_mr"][:, 0:NMAX], op=ALU.subtract)
                    ins = v.tensor_copy(out=xl[i][:, t0:t0 + NMAX], in_=xchunk[db + i][:, :])
            ev.inc(ins)
            ctx.aux2.release(st["bj0"], ev)
            ctx.aux2.release(st["bj1"], ev)
        ap_marks.append(ev.count)

        if f32_to is not None:
            @sec(b, "gpsimd")
            def _(g, t0=t0, db=db, c=c):
                g.wait_ge(ev.h, ap_marks[c])
                for i in range(nt):
                    qo[c % 2].inc(g.dma_start(out=f32_to[i * P:(i + 1) * P, t0:t0 + NMAX],
                                              in_=xchunk[db + i][:, :]), 16)
            out_marks.append(qo[c % 2].count)
    return ap_marks



class ResidChunk:
    def __init__(self, b, com, ctx, xdram, c, pre_waits=()):
        self.b, self.ctx, self.xdram, self.c = b, ctx, xdram, c
        self.nt = ctx.nt
        self.db = (c % ctx.nbuf) * ctx.nt
        self.t0 = c * NMAX
        qxc = b.sem(f"q_xc{c % ctx.nbuf}")
        qob = b.sem(f"q_ob{c % 2}")
        qob_all = [b.sem("q_ob0"), b.sem("q_ob1")] if ctx.nbuf == 1 else [qob]
        nt = self.nt

        @sec(b, "sync")
        def _(s, c=c):
            for q_ in qob_all:
                s.wait_ge(q_.h, q_.count)
            for sem, cnt in pre_waits:
                s.wait_ge(sem.h, cnt)
            for i in range(nt):
                qxc.inc(s.dma_start(out=ctx.xchunk[self.db + i][:, :],
                                    in_=xdram[i * P:(i + 1) * P, self.t0:self.t0 + NMAX]), 16)
        self.dma_mark = (qxc, qxc.count)

    def tile(self, ot):
        return self.ctx.xchunk[self.db + ot]

    def writeback(self, wait_sem, wait_cnt):
        b, ctx, nt = self.b, self.ctx, self.nt
        qob = b.sem(f"q_ob{self.c % 2}")

        @sec(b, "gpsimd")
        def _(g):
            g.wait_ge(wait_sem.h, wait_cnt)
            for i in range(nt):
                qob.inc(g.dma_start(out=self.xdram[i * P:(i + 1) * P, self.t0:self.t0 + NMAX],
                                    in_=ctx.xchunk[self.db + i][:, :]), 16)
        return [(b.sem(f"q_ob0"), b.sem(f"q_ob0").count), (b.sem(f"q_ob1"), b.sem(f"q_ob1").count)]



def block_l0ffn(b, com, ctx, *, xdram, wd, tag=None, guards=()):
    tag = tag or f"l0f{b.uid()}"
    T, ntc, nt = ctx.T, ctx.ntc, ctx.nt
    qw = b.fresh_qsem()
    sc = b.sem("g_sc")
    ev = b.sem("g_ev")
    mm = b.sem("g_mm")

    U1 = WRes(b, f"{tag}_U1", 1024, 256)
    V1 = WRes(b, f"{tag}_V1", 256, 4096)
    U2 = WRes(b, f"{tag}_U2", 4096, 256)
    V2 = WRes(b, f"{tag}_V2", 256, 1024)
    b1c = Cols(b, f"{tag}_b1c", 4096)
    b2h = Cols(b, f"{tag}_b2h", 1024)
    b2c = Cols(b, f"{tag}_b2c", 1024)

    @sec(b, "gpsimd")
    def _(g):
        U1.load(g, wd["U1"], qw)
        V1.load(g, wd["V1"], qw)
        U2.load(g, wd["U2"], qw)
        V2.load(g, wd["V2"], qw)
        b1c.load(g, wd["b1c"], qw)
        b2c.load(g, wd["b2c"], qw)
    wtarget = qw.count

    @sec(b, "vector")
    def _(v):
        v.wait_ge(qw.h, wtarget)
        com.wait_ready(v)
        ev.inc(v.tensor_scalar_mul(out=b2h.tile[:, :], in0=b2c.tile[:, :], scalar1=0.5))

    ap_marks = ln_fm(b, com, ctx, xdram=xdram, guards=guards)

    Pb = [b.sbuf(f"{tag}_P{i}", [P, NMAX], BF16) for i in range(2)]
    Hb = [b.sbuf(f"{tag}_H{i}", [P, NMAX], BF16) for i in range(32)]
    Qb = [b.sbuf(f"{tag}_Q{i}", [P, NMAX], BF16) for i in range(2)]

    half = com.const_pp(0.5)
    final = None
    for c in range(ntc):
        t0 = c * NMAX
        d1 = linear_stage(b, com, src_tiles=ctx.xl.tiles, w=U1, pool=ctx.pool,
                          t0=t0, tc=NMAX, mm_sem=mm, done_sem=sc,
                          pre_waits=[(qw, wtarget), (ev, ap_marks[c])])
        run_epilogues(b, "scalar", d1, mm, ctx.pool, sc,
                      lambda e, ot, ps, _t0: e.copy(out=Pb[ot][:, 0:NMAX], in_=ps))
        d2 = linear_stage(b, com, src_tiles=Pb, w=V1, pool=ctx.pool,
                          t0=t0, tc=NMAX, mm_sem=mm, src_t0=0, done_sem=sc,
                          pre_waits=[(sc, sc.count)])
        run_epilogues(b, "scalar", d2, mm, ctx.pool, sc,
                      lambda e, ot, ps, _t0: e.activation(out=Hb[ot][:, 0:NMAX], in_=ps,
                                                     func=AF.Gelu_apprx_tanh,
                                                     bias=b1c.pp(ot)))
        d3 = linear_stage(b, com, src_tiles=Hb, w=U2, pool=ctx.pool,
                          t0=t0, tc=NMAX, mm_sem=mm, src_t0=0, done_sem=sc,
                          pre_waits=[(sc, sc.count)])
        run_epilogues(b, "scalar", d3, mm, ctx.pool, sc,
                      lambda e, ot, ps, _t0: e.copy(out=Qb[ot][:, 0:NMAX], in_=ps))
        rc = ResidChunk(b, com, ctx, xdram, c, pre_waits=[(sc, sc.count)])
        d4 = linear_stage(b, com, src_tiles=Qb, w=V2, pool=ctx.pool,
                          t0=t0, tc=NMAX, mm_sem=mm, src_t0=0, done_sem=ev,
                          pre_waits=[(sc, sc.count)])

        def ep4(e, ot, ps, _t0, rc=rc):
            e.scalar_tensor_tensor(out=rc.tile(ot)[:, :], in0=ps, scalar=half,
                                   in1=rc.tile(ot)[:, :], op0=ALU.mult, op1=ALU.add)
            return e.tensor_scalar_add(out=rc.tile(ot)[:, :], in0=rc.tile(ot)[:, :],
                                       scalar1=b2h.pp(ot))
        run_epilogues(b, "vector", d4, mm, ctx.pool, ev, ep4,
                      extra_waits=[rc.dma_mark, (ev, ev.count)])
        final = rc.writeback(ev, ev.count)
    return final



def precast_bf16(b, g_stream_sem, pairs):
    qw = g_stream_sem

    @sec(b.b if isinstance(b, SubCtx) else b, "gpsimd")
    def _(g):
        for dst, src, rows, cols in pairs:
            step = max(1, (4 << 20) // max(1, cols * 4))
            r = 0
            while r < rows:
                r1 = min(rows, r + step)
                qw.inc(g.dma_start(out=dst[r:r1, :], in_=src[r:r1, :]), 16)
                r = r1
    return qw.count


def linear_stream_stage(b, com, *, src_tiles, wdram_bf, d_in, d_out, pool,
                        t0, tc, mm_sem, done_sem, qp, panels, pre_waits=(),
                        src_t0=None, OG=4, kw_dma_pre=()):
    qpan = [b.sem("qpan0"), b.sem("qpan1")]
    dma_pre = list(kw_dma_pre) if kw_dma_pre else []
    """out[:, t0:t0+tc] = W.T @ src with W streamed in [128, OG*128] panels.
    panels: 2 sbuf tiles [128, OG*128] bf16 (double buffer).
    qp: SWDGE sem for panel DMAs (gpsimd queue).
    Returns descs for run_epilogues."""
    assert tc <= NMAX
    kt = d_in // P
    n_og = (d_out + OG * P - 1) // (OG * P)
    s0 = t0 if src_t0 is None else src_t0
    descs = []
    done_base = done_sem.count
    ep_idx = 0

    for og in range(n_og):
        o0 = og * OG * P
        o1 = min(d_out, o0 + OG * P)
        now = min(OG, (o1 - o0 + P - 1) // P)
        mm_base = mm_sem.count
        panel_marks = []

        @sec(b, "gpsimd")
        def _(g, og=og, o0=o0, o1=o1, mm_base=mm_base):
            for k in range(kt):
                if k < 2:
                    g.wait_ge(mm_sem.h, mm_base)
                    for dsem, dcnt in dma_pre:
                        g.wait_ge(dsem.h, dcnt)
                else:
                    g.wait_ge(mm_sem.h, mm_base + (k - 1) * now)
                qpan[k % 2].inc(g.dma_start(out=panels[k % 2][:, 0:o1 - o0],
                                            in_=wdram_bf[k * P:(k + 1) * P, o0:o1]), 16)
                panel_marks.append((qpan[k % 2], qpan[k % 2].count))

        @sec(b, "tensor")
        def _(t, og=og, o0=o0, o1=o1, now=now):
            nonlocal ep_idx
            for sem, cnt in pre_waits:
                t.wait_ge(sem.h, cnt)
            banks = []
            for j in range(now):
                if pool.next >= pool.n and pool.next - pool.n >= 0 and (og * OG + j) >= pool.n:
                    t.wait_ge(done_sem.h, done_base + (og * OG + j) - pool.n + 1)
                    pool.next += 1
                    bi = (pool.next - 1) % pool.n
                    pool.pending[bi] = None
                    banks.append((bi, pool.banks[bi]))
                else:
                    banks.append(pool.acquire(t))
            for k in range(kt):
                t.wait_ge(panel_marks[k][0].h, panel_marks[k][1])
                for j in range(now):
                    ins = t.matmul(banks[j][1][0:min(P, o1 - o0 - j * P), 0:tc],
                                   panels[k % 2][:, j * P:min(o1 - o0, (j + 1) * P)],
                                   src_tiles[k][:, s0:s0 + tc],
                                   start=(k == 0), stop=(k == kt - 1))
                    mm_sem.inc(ins)
            for j in range(now):
                ot = og * OG + j
                descs.append((ot, banks[j][0],
                              banks[j][1][0:min(P, o1 - o0 - j * P), 0:tc], mm_sem.count, t0))
                ep_idx += 1
    return descs



def block_l1ffn(b, com, ctx, *, xdram, wd, wbf, precast=True, tag=None, guards=()):
    tag = tag or f"l1f{b.uid()}"
    T, ntc, nt = ctx.T, ctx.ntc, ctx.nt
    qw = b.fresh_qsem()
    qwp = b.fresh_qsem()
    qp = None
    sc = b.sem("g_sc")
    ev = b.sem("g_ev")
    mm = b.sem("g_mm")

    s1c = Cols(b, f"{tag}_s1", 4096)
    s2c = Cols(b, f"{tag}_s2", 4096)
    hs1c = Cols(b, f"{tag}_hs1", 4096)
    hs2c = Cols(b, f"{tag}_hs2", 4096)
    ws3c = Cols(b, f"{tag}_ws3", 1024)
    ws3h = Cols(b, f"{tag}_ws3h", 1024)

    @sec(b, "gpsimd")
    def _(g):
        s1c.load(g, wd["s1c"], qw)
        s2c.load(g, wd["s2c"], qw)
        hs1c.load(g, wd["hs1c"], qw)
        hs2c.load(g, wd["hs2c"], qw)
        ws3c.load(g, wd["ws3c"], qw)
    colst = qw.count

    @sec(b, "vector")
    def _(v):
        v.wait_ge(qw.h, colst)
        com.wait_ready(v)
        ev.inc(v.tensor_scalar_mul(out=ws3h.tile[:, :], in0=ws3c.tile[:, :], scalar1=0.5))

    if precast:
        precast_bf16(b, qwp, [
            (wbf["w1"], wd["w1"], 1024, 4096), (wbf["w2"], wd["w2"], 1024, 4096),
            (wbf["h1"], wd["h1"], 4096, 4096), (wbf["h2"], wd["h2"], 4096, 4096),
            (wbf["w3"], wd["w3"], 4096, 1024)])
    wtargets = [(qw, qw.count), (qwp, qwp.count)]

    ap_marks = ln_fm(b, com, ctx, xdram=xdram, guards=guards)

    Hs = [b.sbuf(f"{tag}_Hs{i}", [P, NMAX], BF16) for i in range(32)]
    Hb = [b.sbuf(f"{tag}_Hb{i}", [P, NMAX], BF16) for i in range(32)]
    panels = [b.sbuf(f"{tag}_pan{i}", [P, 4 * P], BF16) for i in range(2)]

    final = None
    for c in range(ntc):
        t0 = c * NMAX
        pw = wtargets + [(ev, ap_marks[c])]
        dA = linear_stream_stage(b, com, src_tiles=ctx.xl.tiles, wdram_bf=wbf["w1"],
                                 d_in=1024, d_out=4096, pool=ctx.pool, t0=t0, tc=NMAX,
                                 mm_sem=mm, done_sem=sc, qp=None, panels=panels,
                                 pre_waits=pw, src_t0=t0, kw_dma_pre=wtargets)
        run_epilogues(b, "scalar", dA, mm, ctx.pool, sc,
                      lambda e, ot, ps, _t0: e.activation(out=Hs[ot][:, 0:NMAX], in_=ps,
                                                     func=AF.Silu, scale=s1c.pp(ot)))
        dB = linear_stream_stage(b, com, src_tiles=ctx.xl.tiles, wdram_bf=wbf["w2"], kw_dma_pre=wtargets,
                                 d_in=1024, d_out=4096, pool=ctx.pool, t0=t0, tc=NMAX,
                                 mm_sem=mm, done_sem=ev, qp=None, panels=panels,
                                 pre_waits=[(sc, sc.count)], src_t0=t0)
        run_epilogues(b, "vector", dB, mm, ctx.pool, ev,
                      lambda e, ot, ps, _t0: e.scalar_tensor_tensor(
                          out=Hb[ot][:, 0:NMAX], in0=ps, scalar=s2c.pp(ot),
                          in1=Hs[ot][:, 0:NMAX], op0=ALU.mult, op1=ALU.mult))
        dC = linear_stream_stage(b, com, src_tiles=Hb, wdram_bf=wbf["h1"], kw_dma_pre=wtargets,
                                 d_in=4096, d_out=4096, pool=ctx.pool, t0=t0, tc=NMAX,
                                 mm_sem=mm, done_sem=sc, qp=None, panels=panels,
                                 pre_waits=[(ev, ev.count)], src_t0=0)
        run_epilogues(b, "scalar", dC, mm, ctx.pool, sc,
                      lambda e, ot, ps, _t0: e.activation(out=Hs[ot][:, 0:NMAX], in_=ps,
                                                     func=AF.Gelu, scale=hs1c.pp(ot)))
        dD = linear_stream_stage(b, com, src_tiles=Hs, wdram_bf=wbf["h2"], kw_dma_pre=wtargets,
                                 d_in=4096, d_out=4096, pool=ctx.pool, t0=t0, tc=NMAX,
                                 mm_sem=mm, done_sem=sc, qp=None, panels=panels,
                                 pre_waits=[(sc, sc.count)], src_t0=0)
        run_epilogues(b, "scalar", dD, mm, ctx.pool, sc,
                      lambda e, ot, ps, _t0: e.activation(out=Hb[ot][:, 0:NMAX], in_=ps,
                                                     func=AF.Gelu, scale=hs2c.pp(ot)))
        rc = ResidChunk(b, com, ctx, xdram, c, pre_waits=[(sc, sc.count)])
        dE = linear_stream_stage(b, com, src_tiles=Hb, wdram_bf=wbf["w3"], kw_dma_pre=wtargets,
                                 d_in=4096, d_out=1024, pool=ctx.pool, t0=t0, tc=NMAX,
                                 mm_sem=mm, done_sem=ev, qp=None, panels=panels,
                                 pre_waits=[(sc, sc.count)], src_t0=0)

        def epE(e, ot, ps, _t0, rc=rc):
            return e.scalar_tensor_tensor(out=rc.tile(ot)[:, :], in0=ps,
                                          scalar=ws3h.pp(ot), in1=rc.tile(ot)[:, :],
                                          op0=ALU.mult, op1=ALU.add)
        run_epilogues(b, "vector", dE, mm, ctx.pool, ev, epE,
                      extra_waits=[rc.dma_mark, (ev, ev.count)])
        final = rc.writeback(ev, ev.count)
    return final





def add_streams(b, com, ctx, *, xdram, ydram, ychunk, guards=(), scale=None):
    nt, ntc = ctx.nt, ctx.ntc
    qyc = b.sem("q_ych")
    ev = b.sem("g_ev")
    vmarks = []
    final = None
    for c in range(ntc):
        db = (c % ctx.nbuf) * nt
        t0 = c * NMAX
        qxc = b.sem(f"q_xc{c % ctx.nbuf}")
        qob = b.sem(f"q_ob{c % 2}")
        qob_all = [b.sem("q_ob0"), b.sem("q_ob1")] if ctx.nbuf == 1 else [qob]

        @sec(b, "sync")
        def _(s, c=c, t0=t0, db=db):
            for q_ in qob_all:
                s.wait_ge(q_.h, q_.count)
            if c >= 1:
                s.wait_ge(ev.h, vmarks[-1])
            for gsem, gcnt in guards:
                s.wait_ge(gsem.h, gcnt)
            for i in range(nt):
                qxc.inc(s.dma_start(out=ctx.xchunk[db + i][:, :],
                                    in_=xdram[i * P:(i + 1) * P, t0:t0 + NMAX]), 16)
            for i in range(nt):
                qyc.inc(s.dma_start(out=ychunk[i][:, :],
                                    in_=ydram[i * P:(i + 1) * P, t0:t0 + NMAX]), 16)
        dma_mark = (qxc, qxc.count, qyc.count)

        @sec(b, "vector")
        def _(v, db=db):
            v.wait_ge(dma_mark[0].h, dma_mark[1])
            v.wait_ge(qyc.h, dma_mark[2])
            for i in range(nt):
                if scale is None:
                    ins = v.tensor_tensor(out=ctx.xchunk[db + i][:, :],
                                          in0=ctx.xchunk[db + i][:, :],
                                          in1=ychunk[i][:, :], op=ALU.add)
                else:
                    ins = v.scalar_tensor_tensor(out=ctx.xchunk[db + i][:, :],
                                                 in0=ychunk[i][:, :],
                                                 scalar=com.const_pp(scale),
                                                 in1=ctx.xchunk[db + i][:, :],
                                                 op0=ALU.mult, op1=ALU.add)
            ev.inc(ins)
        ev_mark = ev.count
        vmarks.append(ev_mark)

        @sec(b, "gpsimd")
        def _(g, t0=t0, db=db):
            g.wait_ge(ev.h, ev_mark)
            for i in range(nt):
                qob.inc(g.dma_start(out=xdram[i * P:(i + 1) * P, t0:t0 + NMAX],
                                    in_=ctx.xchunk[db + i][:, :]), 16)
        final = [(b.sem("q_ob0"), b.sem("q_ob0").count), (b.sem("q_ob1"), b.sem("q_ob1").count)]
    return final



def block_l0attn(b, com, ctx, *, xdram, ydram, attdram, wd, wbf, mqd, mkd,
                 tag=None, guards=()):
    tag = tag or f"l0a{b.uid()}"
    T, ntc, nt = ctx.T, ctx.ntc, ctx.nt
    NH2 = 8
    qw = b.fresh_qsem()
    qp = None
    sc = b.sem("g_sc")
    ev = b.sem("g_ev")
    mm = b.sem("g_mm")
    qs = b.sem("q_sync")

    Qg = b.sbuf(f"{tag}_Qg", [P, T], BF16)
    Kg = b.sbuf(f"{tag}_Kg", [P, T], BF16)
    Vga = [b.sbuf(f"{tag}_VgA{i}", [P, 65], BF16) for i in range(T // P)]
    Vgb = [b.sbuf(f"{tag}_VgB{i}", [P, 65], BF16) for i in range(T // P)]
    PtA = ctx.bigscr
    Pt = [PtA[:, i * NMAX:(i + 1) * NMAX] for i in range(T // P)]
    sqc = b.sbuf(f"{tag}_sqc", [P, NMAX], BF16)
    mqt = [b.sbuf(f"{tag}_mq{i}", [P, NMAX], BF16) for i in range(2)]
    mkt = [b.sbuf(f"{tag}_mk{i}", [P, NMAX], BF16) for i in range(2)]
    wslice = [b.sbuf(f"{tag}_ws{i}", [P, 3 * P], BF16) for i in range(8)]
    att_sb = [b.sbuf(f"{tag}_att{i}", [64, NMAX], BF16) for i in range(2)]
    rsb = b.sbuf(f"{tag}_rsb", [64, NMAX], BF16)
    brow = b.sbuf(f"{tag}_brow", [1, NMAX + 32], F32)
    shp = b.sbuf(f"{tag}_shp", [P, 2], F32)
    hbuf = [PtA[:, i * NMAX:(i + 1) * NMAX] for i in range(32)]

    @sec(b, "vector")
    def _(v):
        for tt in range(T // P):
            v.memset(Vga[tt][:, 64:65], 1.0)
            ins = v.memset(Vgb[tt][:, 64:65], 1.0)
        ev.inc(ins)

    ln1g = Cols(b, f"{tag}_l1g", 1024)
    ln1b = Cols(b, f"{tag}_l1b", 1024)
    ln2g = Cols(b, f"{tag}_l2g", 1024)
    ln2b = Cols(b, f"{tag}_l2b", 1024)
    bqc = Cols(b, f"{tag}_bq", 1024)
    bkc = Cols(b, f"{tag}_bk", 1024)
    b1c = Cols(b, f"{tag}_b1", 4096)
    b2c = Cols(b, f"{tag}_b2", 1024)
    bvr = b.sbuf(f"{tag}_bvr", [1, 1024], F32)
    y_for_layer = ydram
    for layer in range(2):
        wl = wd[layer]
        wb = wbf[layer]
        qpre = b.fresh_qsem()
        precast_bf16(b, qpre, [(wb["wq"], wl["wq"], 1024, 1024),
                             (wb["wk"], wl["wk"], 1024, 1024),
                             (wb["wv"], wl["wv"], 1024, 1024),
                             (wb["wo"], wl["wo"], 1024, 1024),
                             (wb["w1"], wl["w1"], 1024, 4096),
                             (wb["w2"], wl["w2"], 4096, 1024)])
        wt_pre = (qpre, qpre.count)

        qw = b.fresh_qsem()

        @sec(b, "gpsimd")
        def _(g, wl=wl, layer=layer):
            if layer > 0:
                g.wait_ge(ev.h, ev.count)
                g.wait_ge(sc.h, sc.count)
                g.wait_ge(mm.h, mm.count)
            for cobj, key in ((ln1g, "ln1g"), (ln1b, "ln1b"), (ln2g, "ln2g"),
                              (ln2b, "ln2b"), (bqc, "bqc"), (bkc, "bkc"),
                              (b1c, "b1c"), (b2c, "b2c")):
                cobj.load(g, wl[key], qw)
            qw.inc(g.dma_start(out=bvr[:, :], in_=wl["bvrow"][:, :]), 16)
        wt_l = qw.count

        if layer == 0:
            lay_guards = list(guards)
        else:
            lay_guards = [(b.sem("q_ob0"), b.sem("q_ob0").count),
                          (b.sem("q_ob1"), b.sem("q_ob1").count)]
        ap1 = ln_fm(b, com, ctx, xdram=y_for_layer, gain=ln1g, bias=ln1b,
                    guards=lay_guards + [(qw, wt_l)])
        xl_done = (ev, b.sem("g_ev").count)
        att_parity_marks = {}

        for hg in range(NH2):
            hc0 = hg * P
            qwsl = b.sem(f"{tag}_qwsl")

            @sec(b, "gpsimd")
            def _(g, hc0=hc0, layer=layer, hg=hg):
                g.wait_ge(mm.h, mm.count)
                if hg == 0:
                    g.wait_ge(wt_pre[0].h, wt_pre[1])
                for k in range(8):
                    qwsl.inc(g.dma_start(out=wslice[k][:, 0:P], in_=wb["wq"][k * P:(k + 1) * P, hc0:hc0 + P]), 16)
                    qwsl.inc(g.dma_start(out=wslice[k][:, P:2 * P], in_=wb["wk"][k * P:(k + 1) * P, hc0:hc0 + P]), 16)
                    qwsl.inc(g.dma_start(out=wslice[k][:, 2 * P:3 * P], in_=wb["wv"][k * P:(k + 1) * P, hc0:hc0 + P]), 16)
            wsl_mark = qwsl.count

            qkv_done_marks = []
            for cc in range(ntc):
                t0 = cc * NMAX

                qrot = b.sem(f"qrot{cc % 2}")

                @sec(b, "gpsimd")
                def _(g, cc=cc, t0=t0):
                    g.wait_ge(ev.h, ev.count)
                    qrot.inc(g.dma_start(out=mqt[cc % 2][:, :], in_=mqd[:, t0:t0 + NMAX]), 16)
                    qrot.inc(g.dma_start(out=mkt[cc % 2][:, :], in_=mkd[:, t0:t0 + NMAX]), 16)
                rot_mark = qrot.count

                @sec(b, "tensor")
                def _(t, cc=cc, t0=t0):
                    if cc == 0:
                        t.wait_ge(qwsl.h, wsl_mark)
                        t.wait_ge(ev.h, ap1[-1])
                        t.wait_ge(qw.h, wt_l)
                    for which in range(2):
                        bi, bank = ctx.pool.acquire(t)
                        for k in range(8):
                            ins = t.matmul(bank[:, 0:NMAX],
                                           wslice[k][:, which * P:(which + 1) * P],
                                           ctx.xl.tiles[k][:, t0:t0 + NMAX],
                                           start=(k == 0), stop=(k == 7))
                        mm.inc(ins)
                        st_mark = mm.count

                        @sec(b, "vector")
                        def _(v, which=which, bi=bi, bank=bank, t0=t0, cc=cc, st_mark=st_mark):
                            v.wait_ge(mm.h, st_mark)
                            v.wait_ge(qrot.h, rot_mark)
                            dst = Qg if which == 0 else Kg
                            rot = mqt[cc % 2] if which == 0 else mkt[cc % 2]
                            bias = bqc if which == 0 else bkc
                            ins = v.scalar_tensor_tensor(out=dst[:, t0:t0 + NMAX],
                                                         in0=bank[:, 0:NMAX],
                                                         scalar=bias.pp(hg), in1=rot[:, :],
                                                         op0=ALU.add, op1=ALU.mult)
                            ev.inc(ins)
                        ctx.pool.release(bi, ev)
                    for tti in range(NMAX // P):
                        tt = (t0 // P) + tti
                        bi, bank = ctx.pool.acquire(t)
                        for k in range(8):
                            t.matmul(bank[:, 0:P], ctx.xl.tiles[k][:, tt * P:(tt + 1) * P],
                                     wslice[k][:, 2 * P:3 * P], start=(k == 0), stop=False)
                        ins = t.matmul(bank[:, 0:P], com.ones_row_f[0:1, :],
                                       bvr[0:1, hc0:hc0 + P], start=False, stop=True)
                        mm.inc(ins)
                        st_mark = mm.count

                        @sec(b, "vector")
                        def _(v, tt=tt, bank=bank, st_mark=st_mark):
                            v.wait_ge(mm.h, st_mark)
                            v.tensor_copy(out=Vga[tt][:, 0:64], in_=bank[:, 0:64])
                            ins = v.tensor_copy(out=Vgb[tt][:, 0:64], in_=bank[:, 64:128])
                            ev.inc(ins, 2) if False else ev.inc(ins)
                        ctx.pool.release(bi, ev)
                qkv_done_marks.append(ev.count)

            BL = lambda j: brow[0:1, 0:NMAX]
            sq_stat_marks = []
            for which in range(2):
                src_g = Qg if which == 0 else Kg
                for half in range(2):
                    @sec(b, "scalar")
                    def _(s, which=which, half=half):
                        s.wait_ge(ev.h, qkv_done_marks[-1])
                        s.nop()
                    for cc in range(ntc):
                        @sec(b, "scalar")
                        def _(s, cc=cc, src_g=src_g, half=half):
                            if sq_stat_marks:
                                s.wait_ge(mm.h, sq_stat_marks[-1])
                            ins = s.activation(out=sqc[0:64, :],
                                               in_=src_g[64 * half:64 * half + 64, cc * NMAX:(cc + 1) * NMAX],
                                               func=AF.Square)
                            sc.inc(ins)
                        sc_mark = sc.count

                        @sec(b, "tensor")
                        def _(t, cc=cc):
                            t.wait_ge(sc.h, sc_mark)
                            bi, bank = ctx.aux.acquire(t)
                            mm.inc(t.matmul(bank[0:1, 0:NMAX], com.ones_col[0:64, :],
                                            sqc[0:64, :], start=True, stop=True))
                            sq_stat_marks.append(mm.count)
                            st2 = {}
                            st2["bi"], st2["bank"] = bi, bank

                            @sec(b, "vector")
                            def _(v, cc=cc, bank=bank, which=which, half=half):
                                v.wait_ge(mm.h, mm.count)
                                j = 2 * which + half
                                if cc == 0:
                                    ins = v.reduce_max(out=brow[0:1, NMAX + j:NMAX + j + 1],
                                                       in_=bank[0:1, 0:NMAX], axis=AX.X)
                                else:
                                    v.reduce_max(out=brow[0:1, NMAX + 8 + j:NMAX + 9 + j],
                                                 in_=bank[0:1, 0:NMAX], axis=AX.X)
                                    ins = v.tensor_tensor(out=brow[0:1, NMAX + j:NMAX + j + 1],
                                                          in0=brow[0:1, NMAX + j:NMAX + j + 1],
                                                          in1=brow[0:1, NMAX + 8 + j:NMAX + 9 + j],
                                                          op=ALU.max)
                                ev.inc(ins)
                            ctx.aux.release(bi, ev)
            for half in range(2):
                @sec(b, "vector")
                def _(v, half=half):
                    jq = NMAX + 2 * 0 + half
                    jk = NMAX + 2 * 1 + half
                    v.tensor_tensor(out=brow[0:1, NMAX + 16:NMAX + 17],
                                    in0=brow[0:1, jq:jq + 1], in1=brow[0:1, jk:jk + 1], op=ALU.add)
                    ins = v.tensor_scalar(out=brow[0:1, NMAX + 16:NMAX + 17],
                                          in0=brow[0:1, NMAX + 16:NMAX + 17],
                                          scalar1=-0.5, scalar2=40.0,
                                          op0=ALU.mult, op1=ALU.add)
                    ev.inc(ins)
                ev_m = ev.count

                @sec(b, "tensor")
                def _(t, half=half):
                    t.wait_ge(ev.h, ev_m)
                    bi, bank = ctx.aux.acquire(t)
                    mm.inc(t.matmul(bank[:, 0:1], com.ones_row_f[0:1, :],
                                    brow[0:1, NMAX + 16:NMAX + 17], start=True, stop=True))
                    st3 = {"bi": bi, "bank": bank}

                    @sec(b, "vector")
                    def _(v, half=half, bank=bank):
                        v.wait_ge(mm.h, mm.count)
                        ev.inc(v.tensor_copy(out=shp[:, half:half + 1], in_=bank[:, 0:1]))
                    ctx.aux.release(bi, ev)
            shifts_done = ev.count

            st4 = {}
            for half in range(2):
                h64 = 64 * half
                Vaug = Vga if half == 0 else Vgb
                for qc in range(ntc):
                    q0 = qc * NMAX
                    sc_exp_marks = []
                    for rnd in range(8):
                        @sec(b, "tensor")
                        def _(t, rnd=rnd, half=half, q0=q0):
                            if rnd == 0:
                                t.wait_ge(ev.h, shifts_done)
                            else:
                                t.wait_ge(sc.h, sc.count)
                            for j in range(4):
                                kt = rnd * 4 + j
                                ins = t.matmul(ctx.pool.banks[j][:, 0:NMAX],
                                               Kg[h64:h64 + 64, kt * P:(kt + 1) * P],
                                               Qg[h64:h64 + 64, q0:q0 + NMAX],
                                               start=True, stop=True)
                            mm.inc(ins)
                        mm_mark = mm.count

                        @sec(b, "scalar")
                        def _(s, rnd=rnd, half=half):
                            s.wait_ge(mm.h, mm_mark)
                            ins = s.activation(out=PtA[:, rnd * 4 * NMAX:(rnd + 1) * 4 * NMAX],
                                               in_=ctx.pool.arena[:, 0:4 * NMAX],
                                               func=AF.Exp, bias=shp[:, half:half + 1])
                            sc.inc(ins)
                        sc_exp_marks.append(sc.count)
                    @sec(b, "tensor")
                    def _(t, half=half, q0=q0):
                        t.wait_ge(sc.h, sc_exp_marks[-1])
                        bi, bank = ctx.aux.acquire(t)
                        for kt in range(T // P):
                            ins = t.matmul(bank[0:65, 0:NMAX], Vaug[kt][:, :],
                                           Pt[kt][:, :], start=(kt == 0), stop=(kt == T // P - 1))
                        mm.inc(ins)
                        st4["bi"], st4["bank"] = bi, bank
                    pv_mark = mm.count

                    @sec(b, "vector")
                    def _(v):
                        v.wait_ge(mm.h, pv_mark)
                        ev.inc(v.reciprocal(out=BL(3),
                                            in_=st4["bank"][64:65, 0:NMAX]))
                    rs_mark = ev.count

                    @sec(b, "tensor")
                    def _(t):
                        t.wait_ge(ev.h, rs_mark)
                        bj, bank2 = ctx.aux2.acquire(t)
                        mm.inc(t.matmul(bank2[0:64, 0:NMAX], com.ones_row_f[0:1, 0:64],
                                        BL(3), start=True, stop=True))
                        st4["bj"], st4["bank2"] = bj, bank2
                    bc_mark = mm.count

                    prev_dma = att_parity_marks.get(qc % 2)

                    @sec(b, "vector")
                    def _(v, half=half, qc=qc, prev_dma=prev_dma):
                        v.wait_ge(mm.h, bc_mark)
                        if prev_dma is not None:
                            v.wait_ge(b.sem("q_att").h, prev_dma)
                        v.tensor_copy(out=rsb[:, :], in_=st4["bank2"][0:64, 0:NMAX])
                        ins = v.tensor_tensor(out=att_sb[qc % 2][:, :],
                                              in0=st4["bank"][0:64, 0:NMAX],
                                              in1=rsb[:, :], op=ALU.mult)
                        ev.inc(ins)
                    att_mark = ev.count
                    ctx.aux.release(st4["bi"], ev)
                    ctx.aux2.release(st4["bj"], ev)

                    @sec(b, "gpsimd")
                    def _(g, hg=hg, half=half, q0=q0, qc=qc):
                        g.wait_ge(ev.h, att_mark)
                        h = hg * 2 + half
                        b.sem("q_att").inc(g.dma_start(
                            out=attdram[h * 64:(h + 1) * 64, q0:q0 + NMAX],
                            in_=att_sb[qc % 2][:, :]), 16)
                    att_parity_marks[qc % 2] = b.sem("q_att").count
        att_all = b.sem("q_att").count

        attb = [b.sbuf(f"{tag}_attb{i}", [P, NMAX], BF16) for i in range(8)] if layer == 0 else attb
        wo_panels = [b.sbuf(f"{tag}_wop{i}", [P, 4 * P], BF16) for i in range(2)] if layer == 0 else wo_panels
        for c in range(ntc):
            t0 = c * NMAX

            qattb = b.sem(f"{tag}_qattb")

            @sec(b, "sync")
            def _(s, c=c, t0=t0):
                s.wait_ge(b.sem("q_att").h, att_all)
                if c >= 1:
                    s.wait_ge(mm.h, mm.count)
                for i in range(nt):
                    qattb.inc(s.dma_start(out=attb[i][:, :], in_=attdram[i * P:(i + 1) * P, t0:t0 + NMAX]), 16)
            attb_mark = qattb.count
            rc = ResidChunk(b, com, ctx, y_for_layer, c, pre_waits=[(ev, ev.count)])
            dwo = linear_stream_stage(b, com, src_tiles=attb, wdram_bf=wb["wo"], kw_dma_pre=[wt_pre],
                                      d_in=1024, d_out=1024, pool=ctx.pool, t0=t0, tc=NMAX,
                                      mm_sem=mm, done_sem=ev, qp=None, panels=wo_panels,
                                      pre_waits=[(qattb, attb_mark)], src_t0=0)

            def epwo(e, ot, ps, _t0, rc=rc):
                return e.tensor_tensor(out=rc.tile(ot)[:, :], in0=ps,
                                       in1=rc.tile(ot)[:, :], op=ALU.add)
            run_epilogues(b, "vector", dwo, mm, ctx.pool, ev, epwo,
                          extra_waits=[rc.dma_mark])
            rc.writeback(ev, ev.count)

        ap2 = ln_fm(b, com, ctx, xdram=y_for_layer, gain=ln2g, bias=ln2b,
                    guards=[(b.sem("q_ob0"), b.sem("q_ob0").count),
                            (b.sem("q_ob1"), b.sem("q_ob1").count)])
        ffn_panels = wo_panels
        for c in range(ntc):
            t0 = c * NMAX
            dH = linear_stream_stage(b, com, src_tiles=ctx.xl.tiles, wdram_bf=wb["w1"], kw_dma_pre=[wt_pre],
                                     d_in=1024, d_out=4096, pool=ctx.pool, t0=t0, tc=NMAX,
                                     mm_sem=mm, done_sem=sc, qp=None, panels=ffn_panels,
                                     pre_waits=[(ev, ap2[c]), wt_pre], src_t0=t0)
            run_epilogues(b, "scalar", dH, mm, ctx.pool, sc,
                          lambda e, ot, ps, _t0: e.activation(
                              out=hbuf[ot][:, :], in_=ps,
                              func=AF.Gelu_apprx_sigmoid, bias=b1c.pp(ot)))
            rc = ResidChunk(b, com, ctx, y_for_layer, c, pre_waits=[(sc, sc.count)])
            dO = linear_stream_stage(b, com, src_tiles=hbuf, wdram_bf=wb["w2"], kw_dma_pre=[wt_pre],
                                     d_in=4096, d_out=1024, pool=ctx.pool, t0=t0, tc=NMAX,
                                     mm_sem=mm, done_sem=ev, qp=None, panels=ffn_panels,
                                     pre_waits=[(sc, sc.count)], src_t0=0)

            def epO(e, ot, ps, _t0, rc=rc):
                e.tensor_tensor(out=rc.tile(ot)[:, :], in0=ps, in1=rc.tile(ot)[:, :],
                                op=ALU.add)
                return e.tensor_scalar_add(out=rc.tile(ot)[:, :], in0=rc.tile(ot)[:, :],
                                           scalar1=b2c.pp(ot))
            run_epilogues(b, "vector", dO, mm, ctx.pool, ev, epO,
                          extra_waits=[rc.dma_mark])
            rc.writeback(ev, ev.count)
    return [(b.sem("q_ob0"), b.sem("q_ob0").count), (b.sem("q_ob1"), b.sem("q_ob1").count)]


class SplitF32:

    def __init__(self, tiles):
        self.halves = [t[:, :].bitcast(F32) for t in tiles]
        self.hc = self.halves[0].shape[-1] if hasattr(self.halves[0], 'shape') else 2048
        self.hc = 2048

    def sl(self, c0, n):
        h = c0 // self.hc
        o = c0 - h * self.hc
        assert o + n <= self.hc
        return self.halves[h][:, o:o + n]

    def whole(self):
        return None



def mamba_scan(b, com, ctx, *, T, get_u, get_delta, Brow, Crow, Acols, Dcols,
               y_cb, tag, rows_ready, scr_f32a, scr_f32b, scr_f32c, hping,
               a_ready=None):
    sc = b.sem("g_sc")
    ev = b.sem("g_ev")
    mm = b.sem("g_mm")
    ntc = T // NMAX
    for dt_i in range(16):
        u_ap, u_rdy = get_u(dt_i)
        d_ap, d_rdy = get_delta(dt_i)
        w_ap = scr_f32a
        y_ap = scr_f32b
        dA_ap = scr_f32c

        @sec(b, "vector")
        def _(v, u_ap=u_ap, d_ap=d_ap):
            v.wait_ge(u_rdy[0].h, u_rdy[1])
            v.wait_ge(d_rdy[0].h, d_rdy[1])
            v.tensor_tensor(out=w_ap[:, :], in0=u_ap[:, :], in1=d_ap[:, :], op=ALU.mult)
            v.memset(y_ap.sl(0, 2048), 0.0)
            ev.inc(v.memset(y_ap.sl(2048, 2048), 0.0))
        w_mark = ev.count

        for n in range(16):
            @sec(b, "scalar")
            def _(s, d_ap=d_ap, n=n, dt_i=dt_i, evn=ev.count):
                s.wait_ge(ev.h, evn)
                if a_ready is not None and dt_i == 0 and n == 0:
                    s.wait_ge(a_ready[0].h, a_ready[1])
                s.activation(out=dA_ap.sl(0, 2048), in_=d_ap[:, 0:2048], func=AF.Exp,
                             scale=Acols(dt_i, n))
                sc.inc(s.activation(out=dA_ap.sl(2048, 2048), in_=d_ap[:, 2048:4096],
                                    func=AF.Exp, scale=Acols(dt_i, n)))
            dA_mark = sc.count
            for c in range(ntc):
                t0 = c * NMAX
                stn = {}

                @sec(b, "tensor")
                def _(t, n=n, t0=t0, c=c):
                    if c == 0:
                        t.wait_ge(rows_ready[0].h, rows_ready[1])
                        t.wait_ge(sc.h, dA_mark)
                    stn["bi"], stn["bank"] = ctx.aux.acquire(t)
                    t.matmul(stn["bank"][:, 0:NMAX], com.sel16[:, n * P:(n + 1) * P],
                             Brow[0:16, t0:t0 + NMAX], start=True, stop=True)
                    stn["bj"], stn["bank2"] = ctx.aux2.acquire(t)
                    mm.inc(t.matmul(stn["bank2"][:, 0:NMAX], com.sel16[:, n * P:(n + 1) * P],
                                    Crow[0:16, t0:t0 + NMAX], start=True, stop=True))
                bc_mark = mm.count

                @sec(b, "vector")
                def _(v, c=c, t0=t0, n=n):
                    v.wait_ge(mm.h, bc_mark)
                    hc = hping[:, (c % 2) * NMAX:(c % 2) * NMAX + NMAX]
                    v.tensor_tensor(out=hc, in0=w_ap[:, t0:t0 + NMAX],
                                    in1=stn["bank"][:, 0:NMAX], op=ALU.mult)
                    if c == 0:
                        ini = 0.0
                    else:
                        ini = hping[:, ((c - 1) % 2) * NMAX + NMAX - 1:((c - 1) % 2) * NMAX + NMAX]
                    v.tensor_tensor_scan(out=hc, data0=dA_ap.sl(t0, NMAX),
                                         data1=hc, initial=ini,
                                         op0=ALU.mult, op1=ALU.add)
                    v.tensor_tensor(out=stn["bank2"][:, 0:NMAX], in0=hc,
                                    in1=stn["bank2"][:, 0:NMAX], op=ALU.mult)
                    ins = v.tensor_tensor(out=y_ap.sl(t0, NMAX), in0=y_ap.sl(t0, NMAX),
                                          in1=stn["bank2"][:, 0:NMAX], op=ALU.add)
                    ev.inc(ins)
                ctx.aux.release(stn["bi"], ev)
                ctx.aux2.release(stn["bj"], ev)
        @sec(b, "vector")
        def _(v, u_ap=u_ap, dt_i=dt_i):
            v.scalar_tensor_tensor(out=y_ap.sl(0, 2048), in0=u_ap[:, 0:2048],
                                   scalar=Dcols(dt_i), in1=y_ap.sl(0, 2048),
                                   op0=ALU.mult, op1=ALU.add)
            ins = v.scalar_tensor_tensor(out=y_ap.sl(2048, 2048), in0=u_ap[:, 2048:4096],
                                         scalar=Dcols(dt_i), in1=y_ap.sl(2048, 2048),
                                         op0=ALU.mult, op1=ALU.add)
            ev.inc(ins)
        y_cb(dt_i, y_ap, (ev, ev.count))



def block_l2attn(b, com, ctx, *, xdram, wd, wbf, xzdram, ygdram, udram,
                 tag=None, guards=()):
    tag = tag or f"l2a{b.uid()}"
    T, ntc, nt = ctx.T, ctx.ntc, ctx.nt
    qw = b.fresh_qsem()
    qpre = b.fresh_qsem()
    qxz = b.fresh_qsem()
    sc = b.sem("g_sc")
    ev = b.sem("g_ev")
    mm = b.sem("g_mm")

    cwc = Cols(b, f"{tag}_cw", 64 * P)
    cbc = Cols(b, f"{tag}_cb", 16 * P)
    Ac = Cols(b, f"{tag}_A", 256 * P)
    Dc = Cols(b, f"{tag}_D", 16 * P)
    WxB = WRes(b, f"{tag}_wxb", 2048, 16)
    WxC = WRes(b, f"{tag}_wxc", 2048, 16)
    Wxd = WRes(b, f"{tag}_wxd", 2048, 1)

    @sec(b, "gpsimd")
    def _(g):
        cwc.load(g, wd["conv_cw"], qw)
        cbc.load(g, wd["conv_cb"], qw)
        Ac.load(g, wd["A_log_cols"], qw)
        Dc.load(g, wd["D_cols"], qw)
        WxB.load(g, wd["WxB"], qw)
        WxC.load(g, wd["WxC"], qw)
        Wxd.load(g, wd["Wxd"], qw)
    wt_cols = qw.count

    @sec(b, "scalar")
    def _(s):
        s.wait_ge(qw.h, wt_cols)
        sc.inc(s.activation(out=Ac.tile[:, :], in_=Ac.tile[:, :], func=AF.Exp))
    a_exp = sc.count

    @sec(b, "vector")
    def _(v):
        v.wait_ge(sc.h, a_exp)
        ev.inc(v.tensor_scalar_mul(out=Ac.tile[:, :], in0=Ac.tile[:, :], scalar1=-1.0))
    a_ready = (ev, ev.count)
    precast_bf16(b, qpre, [(wbf["W_in"], wd["W_inT"], 1024, 4096),
                           (wbf["W_out"], wd["W_outT"], 2048, 1024)])
    wt_pre = (qpre, qpre.count)

    ap_marks = ln_fm(b, com, ctx, xdram=xdram, guards=guards)

    panels = [b.sbuf(f"{tag}_pan{i}", [P, 4 * P], BF16) for i in range(2)]
    xz_sb = [b.sbuf(f"{tag}_xz{i}", [P, NMAX], BF16) for i in range(4)]

    xz_ot_marks = {}
    for c in range(ntc):
        t0 = c * NMAX
        dxz = linear_stream_stage(b, com, src_tiles=ctx.xl.tiles, wdram_bf=wbf["W_in"],
                                  d_in=1024, d_out=4096, pool=ctx.pool, t0=t0, tc=NMAX,
                                  mm_sem=mm, done_sem=sc, qp=None, panels=panels,
                                  pre_waits=[(ev, ap_marks[c]), wt_pre], src_t0=t0,
                                  kw_dma_pre=[wt_pre])
        for ot, bank_i, ps, mark, t0_ in dxz:
            @sec(b, "scalar")
            def _(s, ot=ot, ps=ps, mark=mark):
                s.wait_ge(mm.h, mark)
                if (ot, 1) in xz_ot_marks:
                    pass
                prev = xz_ot_marks.get(ot % 4)
                if prev is not None:
                    s.wait_ge(qxz.h, prev)
                sc.inc(s.copy(out=xz_sb[ot % 4][:, :], in_=ps))
            ctx.pool.release(bank_i, sc)
            sc_m = sc.count

            @sec(b, "sync")
            def _(s, ot=ot, t0=t0, sc_m=sc_m):
                s.wait_ge(sc.h, sc_m)
                qxz.inc(s.dma_start(out=xzdram[ot * P:(ot + 1) * P, t0:t0 + NMAX],
                                    in_=xz_sb[ot % 4][:, :]), 16)
            xz_ot_marks[ot % 4] = qxz.count
    xz_done = (qxz, qxz.count)

    xl0 = ctx.xl
    u_sb = xl0.tiles[0]
    xc_sb = xl0.tiles[1]
    z_sb = xl0.tiles[2]
    yg_sb = xl0.tiles[3]
    uslc = [xl0.tiles[4][:, i * NMAX:(i + 1) * NMAX] for i in range(8)] + \
           [xl0.tiles[5][:, i * NMAX:(i + 1) * NMAX] for i in range(8)]
    big32 = ctx.bigscr[:, :].bitcast(F32)
    scrA = big32[:, 0:T]
    dbc = big32[:, T:2 * T]
    yt_f32 = SplitF32([xl0.tiles[6], xl0.tiles[7]])
    dAt_t = [b.sbuf(f"{tag}_dA{i}", [P, T], BF16) for i in range(2)]
    dAt = SplitF32(dAt_t)
    hping = b.sbuf(f"{tag}_hp", [P, 2 * NMAX + 2], F32)
    Brow = b.sbuf(f"{tag}_Br", [16, T], BF16)
    Crow = b.sbuf(f"{tag}_Cr", [16, T], BF16)
    drow = b.sbuf(f"{tag}_dr", [1, T], F32)
    qu = b.fresh_qsem()
    quw = b.fresh_qsem()

    u_w_marks = []
    for dt_i in range(16):
        @sec(b, "sync")
        def _(s, dt_i=dt_i):
            s.wait_ge(qxz.h, xz_done[1])
            if dt_i > 0:
                s.wait_ge(sc.h, sc.count)
            qu.inc(s.dma_start(out=xc_sb[:, :], in_=xzdram[dt_i * P:(dt_i + 1) * P, :]), 16)
        ld_mark = qu.count

        @sec(b, "vector")
        def _(v, dt_i=dt_i):
            v.wait_ge(qu.h, ld_mark)
            if dt_i > 0:
                v.wait_ge(quw.h, quw.count)
            v.memset(scrA[:, 0:T], 0.0)
            for w in range(4):
                sh = 3 - w
                ins = v.scalar_tensor_tensor(out=scrA[:, sh:T], in0=xc_sb[:, 0:T - sh],
                                             scalar=cwc.pp(dt_i * 4 + w), in1=scrA[:, sh:T],
                                             op0=ALU.mult, op1=ALU.add)
            ev.inc(ins)
        conv_mark = ev.count

        @sec(b, "scalar")
        def _(s, dt_i=dt_i):
            s.wait_ge(ev.h, conv_mark)
            if dt_i == 0:
                s.wait_ge(qw.h, wt_cols)
            sc.inc(s.activation(out=u_sb[:, :], in_=scrA[:, 0:T], func=AF.Silu,
                                bias=cbc.pp(dt_i)))
        u_mark = sc.count

        @sec(b, "gpsimd")
        def _(g, dt_i=dt_i, u_mark=u_mark):
            g.wait_ge(sc.h, u_mark)
            quw.inc(g.dma_start(out=udram[dt_i * P:(dt_i + 1) * P, :], in_=u_sb[:, :]), 16)
        u_w_marks.append(quw.count)

    qus = b.fresh_qsem()
    for c in range(ntc):
        t0 = c * NMAX

        @sec(b, "sync")
        def _(s, c=c, t0=t0):
            s.wait_ge(quw.h, u_w_marks[-1])
            if c > 0:
                s.wait_ge(mm.h, mm.count)
            for dt_i in range(16):
                qus.inc(s.dma_start(out=uslc[dt_i][:, :],
                                    in_=udram[dt_i * P:(dt_i + 1) * P, t0:t0 + NMAX]), 16)
        us_mark = qus.count

        stx = {}

        @sec(b, "tensor")
        def _(t, c=c):
            t.wait_ge(qus.h, us_mark)
            stx["b0"], stx["k0"] = ctx.aux.acquire(t)
            for dt_i in range(16):
                t.matmul(stx["k0"][0:16, 0:NMAX], WxB.tiles[dt_i][:, :], uslc[dt_i][:, :],
                         start=(dt_i == 0), stop=(dt_i == 15))
            stx["b1"], stx["k1"] = ctx.aux.acquire(t)
            for dt_i in range(16):
                t.matmul(stx["k1"][0:16, 0:NMAX], WxC.tiles[dt_i][:, :], uslc[dt_i][:, :],
                         start=(dt_i == 0), stop=(dt_i == 15))
            stx["b2"], stx["k2"] = ctx.aux2.acquire(t)
            for dt_i in range(16):
                ins = t.matmul(stx["k2"][0:1, 0:NMAX], Wxd.tiles[dt_i][:, :], uslc[dt_i][:, :],
                               start=(dt_i == 0), stop=(dt_i == 15))
            mm.inc(ins)
        x_mark = mm.count

        @sec(b, "vector")
        def _(v, c=c, t0=t0):
            v.wait_ge(mm.h, x_mark)
            v.tensor_copy(out=Brow[:, t0:t0 + NMAX], in_=stx["k0"][0:16, 0:NMAX])
            v.tensor_copy(out=Crow[:, t0:t0 + NMAX], in_=stx["k1"][0:16, 0:NMAX])
            ev.inc(v.tensor_copy(out=drow[:, t0:t0 + NMAX], in_=stx["k2"][0:1, 0:NMAX]))
            ctx.aux.release(stx["b0"], ev)
            ctx.aux.release(stx["b1"], ev)
            ctx.aux2.release(stx["b2"], ev)
    rows_raw = (ev, ev.count)

    sp_scr = big32[0:1, T:2 * T + 0] if False else big32[0:1, 0:2 * T]

    @sec(b, "vector")
    def _(v):
        v.wait_ge(rows_raw[0].h, rows_raw[1])
        ev.inc(v.tensor_relu(out=sp_scr[0:1, 0:T], in_=drow[0:1, :]))
    relu_mark = ev.count

    @sec(b, "scalar")
    def _(s):
        s.wait_ge(ev.h, relu_mark)
        s.activation(out=sp_scr[0:1, T:2 * T], in_=drow[0:1, :], func=AF.Abs)
        s.activation(out=sp_scr[0:1, T:2 * T], in_=sp_scr[0:1, T:2 * T], func=AF.Exp, scale=-1.0)
        sc.inc(s.activation(out=sp_scr[0:1, T:2 * T], in_=sp_scr[0:1, T:2 * T], func=AF.Ln,
                            bias=com.const_pp(1.0)[0:1, :]))
    ln_mark = sc.count

    @sec(b, "vector")
    def _(v):
        v.wait_ge(sc.h, ln_mark)
        ev.inc(v.tensor_tensor(out=drow[0:1, :], in0=sp_scr[0:1, 0:T],
                               in1=sp_scr[0:1, T:2 * T], op=ALU.add))
    sp_mark = ev.count

    for c in range(ntc):
        t0 = c * NMAX
        std = {}

        @sec(b, "tensor")
        def _(t, t0=t0, c=c):
            if c == 0:
                t.wait_ge(ev.h, sp_mark)
            std["bi"], std["bank"] = ctx.aux.acquire(t)
            mm.inc(t.matmul(std["bank"][:, 0:NMAX], com.ones_row_f[0:1, :],
                            drow[0:1, t0:t0 + NMAX], start=True, stop=True))
        dbc_mark = mm.count

        @sec(b, "vector")
        def _(v, t0=t0):
            v.wait_ge(mm.h, dbc_mark)
            ev.inc(v.tensor_copy(out=dbc[:, t0:t0 + NMAX], in_=std["bank"][:, 0:NMAX]))
            ctx.aux.release(std["bi"], ev)
    rows_ready = (ev, ev.count)

    qug = b.fresh_qsem()
    qyg = b.fresh_qsem()

    def get_u(dt_i):
        @sec(b, "sync")
        def _(s, dt_i=dt_i):
            s.wait_ge(quw.h, u_w_marks[-1])
            if dt_i > 0:
                s.wait_ge(ev.h, ev.count)
            qug.inc(s.dma_start(out=u_sb[:, :], in_=udram[dt_i * P:(dt_i + 1) * P, :]), 16)
            qug.inc(s.dma_start(out=z_sb[:, :], in_=xzdram[(16 + dt_i) * P:(17 + dt_i) * P, :]), 16)
        return u_sb, (qug, qug.count)

    def get_delta(dt_i):
        return dbc, rows_ready

    ygm = {}

    def y_cb(dt_i, y_ap, rdy):
        @sec(b, "scalar")
        def _(s, dt_i=dt_i):
            s.wait_ge(rdy[0].h, rdy[1])
            sc.inc(s.activation(out=xc_sb[:, :], in_=z_sb[:, :], func=AF.Silu))
        zm = sc.count

        @sec(b, "vector")
        def _(v, dt_i=dt_i, zm=zm):
            v.wait_ge(sc.h, zm)
            if dt_i > 0:
                v.wait_ge(qyg.h, qyg.count)
            v.tensor_tensor(out=yg_sb[:, 0:2048], in0=y_ap.sl(0, 2048),
                            in1=xc_sb[:, 0:2048], op=ALU.mult)
            ev.inc(v.tensor_tensor(out=yg_sb[:, 2048:4096], in0=y_ap.sl(2048, 2048),
                                   in1=xc_sb[:, 2048:4096], op=ALU.mult))
        gm = ev.count

        @sec(b, "gpsimd")
        def _(g, dt_i=dt_i, gm=gm):
            g.wait_ge(ev.h, gm)
            qyg.inc(g.dma_start(out=ygdram[dt_i * P:(dt_i + 1) * P, :], in_=yg_sb[:, :]), 16)
        ygm[dt_i] = qyg.count

    mamba_scan(b, com, ctx, T=T, get_u=get_u, get_delta=get_delta, Brow=Brow,
               Crow=Crow, Acols=lambda dt_i, n: Ac.pp(dt_i * 16 + n),
               Dcols=lambda dt_i: Dc.pp(dt_i), y_cb=y_cb, tag=tag,
               rows_ready=rows_ready, scr_f32a=scrA, scr_f32b=yt_f32,
               scr_f32c=dAt, hping=hping, a_ready=a_ready)
    yg_all = (qyg, qyg.count)

    qys = b.fresh_qsem()
    for c in range(ntc):
        t0 = c * NMAX

        @sec(b, "sync")
        def _(s, c=c, t0=t0):
            s.wait_ge(yg_all[0].h, yg_all[1])
            if c > 0:
                s.wait_ge(mm.h, mm.count)
            for dt_i in range(16):
                qys.inc(s.dma_start(out=uslc[dt_i][:, :],
                                    in_=ygdram[dt_i * P:(dt_i + 1) * P, t0:t0 + NMAX]), 16)
        ys_mark = qys.count
        rc = ResidChunk(b, com, ctx, xdram, c, pre_waits=[(qys, ys_mark)])
        descs = linear_stream_stage(b, com, src_tiles=uslc, wdram_bf=wbf["W_out"],
                                    d_in=2048, d_out=1024, pool=ctx.pool, t0=t0,
                                    tc=NMAX, mm_sem=mm, done_sem=ev, qp=None,
                                    panels=panels, pre_waits=[(qys, ys_mark)],
                                    src_t0=0, kw_dma_pre=[wt_pre])

        def epr(e, ot, ps, _t0, rc=rc):
            return e.tensor_tensor(out=rc.tile(ot)[:, :], in0=ps, in1=rc.tile(ot)[:, :],
                                   op=ALU.add)
        run_epilogues(b, "vector", descs, mm, ctx.pool, ev, epr,
                      extra_waits=[rc.dma_mark])
        final = rc.writeback(ev, ev.count)
    return final



def block_l1attn(b, com, ctx, *, xdram, wd, wbf, xzdram, ygdram, udram,
                 tag=None, guards=()):
    tag = tag or f"l1a{b.uid()}"
    T, ntc, nt = ctx.T, ctx.ntc, ctx.nt
    qw = b.fresh_qsem()
    qpre = b.fresh_qsem()
    qxz = b.fresh_qsem()
    sc = b.sem("g_sc")
    ev = b.sem("g_ev")
    mm = b.sem("g_mm")

    cwc = Cols(b, f"{tag}_cw", 64 * P)
    cbc = Cols(b, f"{tag}_cb", 16 * P)
    Ac = Cols(b, f"{tag}_A", 256 * P)
    Dc = Cols(b, f"{tag}_D", 16 * P)
    dtbc = Cols(b, f"{tag}_dtb", 16 * P)
    fc1b = Cols(b, f"{tag}_f1b", 1024)
    fc2b = Cols(b, f"{tag}_f2b", 1024)
    WxB = WRes(b, f"{tag}_wxb", 2048, 16)
    WxC = WRes(b, f"{tag}_wxc", 2048, 16)
    Wxdl = WRes(b, f"{tag}_wxdl", 2048, 64)
    WDTP = WRes(b, f"{tag}_wdtp", 64, 2048)

    @sec(b, "gpsimd")
    def _(g):
        for cobj, key in ((cwc, "conv_cw"), (cbc, "conv_cb"), (Ac, "A_log_cols"),
                          (Dc, "D_cols"), (dtbc, "dtb_c"), (fc1b, "fc1_bc"),
                          (fc2b, "fc2_bc")):
            cobj.load(g, wd[key], qw)
        WxB.load(g, wd["WxB"], qw)
        WxC.load(g, wd["WxC"], qw)
        Wxdl.load(g, wd["Wxdl"], qw)
        WDTP.load(g, wd["dtprojT"], qw)
    wt_cols = qw.count

    @sec(b, "scalar")
    def _(s):
        s.wait_ge(qw.h, wt_cols)
        sc.inc(s.activation(out=Ac.tile[:, :], in_=Ac.tile[:, :], func=AF.Exp))
    a_exp = sc.count

    @sec(b, "vector")
    def _(v):
        v.wait_ge(sc.h, a_exp)
        ev.inc(v.tensor_scalar_mul(out=Ac.tile[:, :], in0=Ac.tile[:, :], scalar1=-1.0))
    a_ready = (ev, ev.count)
    precast_bf16(b, qpre, [(wbf["in_proj"], wd["in_projT"], 1024, 4096),
                           (wbf["W_out"], wd["W_outT"], 2048, 1024),
                           (wbf["fc1"], wd["fc1T"], 1024, 1024),
                           (wbf["fc2"], wd["fc2T"], 1024, 1024)])
    wt_pre = (qpre, qpre.count)

    panels = [b.sbuf(f"{tag}_pan{i}", [P, 4 * P], BF16) for i in range(2)]
    xz_sb = [b.sbuf(f"{tag}_xz{i}", [P, NMAX], BF16) for i in range(4)]
    ap_marks = ln_fm(b, com, ctx, xdram=xdram, guards=guards)

    h0c = [b.sbuf(f"{tag}_h0c{i}", [P, NMAX], BF16) for i in range(8)]
    xz_ot_marks = {}
    for c in range(ntc):
        t0 = c * NMAX
        d0 = linear_stream_stage(b, com, src_tiles=ctx.xl.tiles, wdram_bf=wbf["fc1"],
                                 d_in=1024, d_out=1024, pool=ctx.pool, t0=t0, tc=NMAX,
                                 mm_sem=mm, done_sem=ev, qp=None, panels=panels,
                                 pre_waits=[(qw, wt_cols), (ev, ap_marks[c]), wt_pre],
                                 src_t0=t0, kw_dma_pre=[wt_pre])

        def eph0(e, ot, ps, t0_):
            return e.tensor_scalar_add(out=h0c[ot][:, :], in0=ps, scalar1=fc1b.pp(ot))
        run_epilogues(b, "vector", d0, mm, ctx.pool, ev, eph0)
        h0_done = (ev, ev.count)
        dxz = linear_stream_stage(b, com, src_tiles=h0c, wdram_bf=wbf["in_proj"],
                                  d_in=1024, d_out=4096, pool=ctx.pool, t0=t0, tc=NMAX,
                                  mm_sem=mm, done_sem=sc, qp=None, panels=panels,
                                  pre_waits=[h0_done, wt_pre], src_t0=0,
                                  kw_dma_pre=[wt_pre])
        for ot, bank_i, ps, mark, t0_ in dxz:
            @sec(b, "scalar")
            def _(s, ot=ot, ps=ps, mark=mark):
                s.wait_ge(mm.h, mark)
                prev = xz_ot_marks.get(ot % 4)
                if prev is not None:
                    s.wait_ge(qxz.h, prev)
                sc.inc(s.copy(out=xz_sb[ot % 4][:, :], in_=ps))
            ctx.pool.release(bank_i, sc)
            sc_m = sc.count

            @sec(b, "sync")
            def _(s, ot=ot, t0=t0, sc_m=sc_m):
                s.wait_ge(sc.h, sc_m)
                qxz.inc(s.dma_start(out=xzdram[ot * P:(ot + 1) * P, t0:t0 + NMAX],
                                    in_=xz_sb[ot % 4][:, :]), 16)
            xz_ot_marks[ot % 4] = qxz.count
    xz_done = (qxz, qxz.count)

    xl0 = ctx.xl
    u_sb = xl0.tiles[0]
    xc_sb = xl0.tiles[1]
    z_sb = xl0.tiles[2]
    yg_sb = xl0.tiles[3]
    uslc = [xl0.tiles[4][:, i * NMAX:(i + 1) * NMAX] for i in range(8)] + \
           [xl0.tiles[5][:, i * NMAX:(i + 1) * NMAX] for i in range(8)]
    big32 = ctx.bigscr[:, :].bitcast(F32)
    scrA = big32[:, 0:T]
    dtt = big32[:, T:2 * T]
    yt_f32 = SplitF32([xl0.tiles[6], xl0.tiles[7]])
    dAt_t = [b.sbuf(f"{tag}_dA{i}", [P, T], BF16) for i in range(2)]
    dAt = SplitF32(dAt_t)
    hping = b.sbuf(f"{tag}_hp", [P, 2 * NMAX + 2], F32)
    Brow = b.sbuf(f"{tag}_Br", [16, T], BF16)
    Crow = b.sbuf(f"{tag}_Cr", [16, T], BF16)
    dtlow = b.sbuf(f"{tag}_dtl", [64, T], BF16)
    qu = b.fresh_qsem()
    quw = b.fresh_qsem()

    u_w_marks = []
    for dt_i in range(16):
        @sec(b, "sync")
        def _(s, dt_i=dt_i):
            s.wait_ge(qxz.h, xz_done[1])
            if dt_i > 0:
                s.wait_ge(sc.h, sc.count)
            qu.inc(s.dma_start(out=xc_sb[:, :], in_=xzdram[dt_i * P:(dt_i + 1) * P, :]), 16)
        ld_mark = qu.count

        @sec(b, "vector")
        def _(v, dt_i=dt_i, ld_mark=ld_mark):
            v.wait_ge(qu.h, ld_mark)
            if dt_i > 0:
                v.wait_ge(quw.h, quw.count)
            v.memset(scrA[:, 0:T], 0.0)
            for w in range(4):
                sh = 3 - w
                ins = v.scalar_tensor_tensor(out=scrA[:, sh:T], in0=xc_sb[:, 0:T - sh],
                                             scalar=cwc.pp(dt_i * 4 + w), in1=scrA[:, sh:T],
                                             op0=ALU.mult, op1=ALU.add)
            ev.inc(ins)
        conv_mark = ev.count

        @sec(b, "scalar")
        def _(s, dt_i=dt_i, conv_mark=conv_mark):
            s.wait_ge(ev.h, conv_mark)
            sc.inc(s.activation(out=u_sb[:, :], in_=scrA[:, 0:T], func=AF.Silu,
                                bias=cbc.pp(dt_i)))
        u_mark = sc.count

        @sec(b, "gpsimd")
        def _(g, dt_i=dt_i, u_mark=u_mark):
            g.wait_ge(sc.h, u_mark)
            quw.inc(g.dma_start(out=udram[dt_i * P:(dt_i + 1) * P, :], in_=u_sb[:, :]), 16)
        u_w_marks.append(quw.count)

    qus = b.fresh_qsem()
    for c in range(ntc):
        t0 = c * NMAX

        @sec(b, "sync")
        def _(s, c=c, t0=t0):
            s.wait_ge(quw.h, u_w_marks[-1])
            if c > 0:
                s.wait_ge(mm.h, mm.count)
            for dt_i in range(16):
                qus.inc(s.dma_start(out=uslc[dt_i][:, :],
                                    in_=udram[dt_i * P:(dt_i + 1) * P, t0:t0 + NMAX]), 16)
        us_mark = qus.count
        stx = {}

        @sec(b, "tensor")
        def _(t, c=c):
            t.wait_ge(qus.h, us_mark)
            stx["b0"], stx["k0"] = ctx.aux.acquire(t)
            for dt_i in range(16):
                t.matmul(stx["k0"][0:16, 0:NMAX], WxB.tiles[dt_i][:, :], uslc[dt_i][:, :],
                         start=(dt_i == 0), stop=(dt_i == 15))
            stx["b1"], stx["k1"] = ctx.aux.acquire(t)
            for dt_i in range(16):
                t.matmul(stx["k1"][0:16, 0:NMAX], WxC.tiles[dt_i][:, :], uslc[dt_i][:, :],
                         start=(dt_i == 0), stop=(dt_i == 15))
            stx["b2"], stx["k2"] = ctx.aux2.acquire(t)
            for dt_i in range(16):
                ins = t.matmul(stx["k2"][0:64, 0:NMAX], Wxdl.tiles[dt_i][:, :], uslc[dt_i][:, :],
                               start=(dt_i == 0), stop=(dt_i == 15))
            mm.inc(ins)
        x_mark = mm.count

        @sec(b, "vector")
        def _(v, c=c, t0=t0):
            v.wait_ge(mm.h, x_mark)
            v.tensor_copy(out=Brow[:, t0:t0 + NMAX], in_=stx["k0"][0:16, 0:NMAX])
            v.tensor_copy(out=Crow[:, t0:t0 + NMAX], in_=stx["k1"][0:16, 0:NMAX])
            ev.inc(v.tensor_copy(out=dtlow[:, t0:t0 + NMAX], in_=stx["k2"][0:64, 0:NMAX]))
            ctx.aux.release(stx["b0"], ev)
            ctx.aux.release(stx["b1"], ev)
            ctx.aux2.release(stx["b2"], ev)
    rows_ready = (ev, ev.count)

    qug = b.fresh_qsem()
    qyg = b.fresh_qsem()

    def get_u(dt_i):
        @sec(b, "sync")
        def _(s, dt_i=dt_i):
            s.wait_ge(quw.h, u_w_marks[-1])
            if dt_i > 0:
                s.wait_ge(ev.h, ev.count)
            qug.inc(s.dma_start(out=u_sb[:, :], in_=udram[dt_i * P:(dt_i + 1) * P, :]), 16)
            qug.inc(s.dma_start(out=z_sb[:, :], in_=xzdram[(16 + dt_i) * P:(17 + dt_i) * P, :]), 16)
        return u_sb, (qug, qug.count)

    def get_delta(dt_i):
        for c in range(ntc):
            t0 = c * NMAX
            std = {}

            @sec(b, "tensor")
            def _(t, c=c, t0=t0, dt_i=dt_i):
                if dt_i == 0 and c == 0:
                    t.wait_ge(rows_ready[0].h, rows_ready[1])
                if c == 0:
                    t.wait_ge(ev.h, ev.count)
                std["bi"], std["bank"] = ctx.aux.acquire(t)
                mm.inc(t.matmul(std["bank"][:, 0:NMAX],
                                WDTP.tiles[0][:, dt_i * P:(dt_i + 1) * P],
                                dtlow[:, t0:t0 + NMAX], start=True, stop=True))
            dtm = mm.count

            @sec(b, "vector")
            def _(v, c=c, t0=t0, dt_i=dt_i):
                v.wait_ge(mm.h, dtm)
                v.tensor_scalar_add(out=dtt[:, t0:t0 + NMAX], in0=std["bank"][:, 0:NMAX],
                                    scalar1=dtbc.pp(dt_i))
                ev.inc(v.tensor_relu(out=scrA[:, t0:t0 + NMAX], in_=dtt[:, t0:t0 + NMAX]))
            rl = ev.count

            @sec(b, "scalar")
            def _(s, c=c, t0=t0):
                s.wait_ge(ev.h, rl)
                s.activation(out=dtt[:, t0:t0 + NMAX], in_=dtt[:, t0:t0 + NMAX], func=AF.Abs)
                s.activation(out=dtt[:, t0:t0 + NMAX], in_=dtt[:, t0:t0 + NMAX], func=AF.Exp,
                             scale=-1.0)
                sc.inc(s.activation(out=dtt[:, t0:t0 + NMAX], in_=dtt[:, t0:t0 + NMAX],
                                    func=AF.Ln, bias=com.const_pp(1.0)))
            lnm = sc.count

            @sec(b, "vector")
            def _(v, c=c, t0=t0):
                v.wait_ge(sc.h, lnm)
                ev.inc(v.tensor_tensor(out=dtt[:, t0:t0 + NMAX], in0=dtt[:, t0:t0 + NMAX],
                                       in1=scrA[:, t0:t0 + NMAX], op=ALU.add))
            ctx.aux.release(std["bi"], ev)
        return dtt, (ev, ev.count)

    ygm = {}

    def y_cb(dt_i, y_ap, rdy):
        @sec(b, "scalar")
        def _(s, dt_i=dt_i):
            s.wait_ge(rdy[0].h, rdy[1])
            sc.inc(s.activation(out=xc_sb[:, :], in_=z_sb[:, :], func=AF.Silu))
        zm = sc.count

        @sec(b, "vector")
        def _(v, dt_i=dt_i, zm=zm):
            v.wait_ge(sc.h, zm)
            if dt_i > 0:
                v.wait_ge(qyg.h, qyg.count)
            v.tensor_tensor(out=yg_sb[:, 0:2048], in0=y_ap.sl(0, 2048),
                            in1=xc_sb[:, 0:2048], op=ALU.mult)
            ev.inc(v.tensor_tensor(out=yg_sb[:, 2048:4096], in0=y_ap.sl(2048, 2048),
                                   in1=xc_sb[:, 2048:4096], op=ALU.mult))
        gm = ev.count

        @sec(b, "gpsimd")
        def _(g, dt_i=dt_i, gm=gm):
            g.wait_ge(ev.h, gm)
            qyg.inc(g.dma_start(out=ygdram[dt_i * P:(dt_i + 1) * P, :], in_=yg_sb[:, :]), 16)
        ygm[dt_i] = qyg.count

    mamba_scan(b, com, ctx, T=T, get_u=get_u, get_delta=get_delta, Brow=Brow,
               Crow=Crow, Acols=lambda dt_i, n: Ac.pp(dt_i * 16 + n),
               Dcols=lambda dt_i: Dc.pp(dt_i), y_cb=y_cb, tag=tag,
               rows_ready=rows_ready, scr_f32a=scrA, scr_f32b=yt_f32,
               scr_f32c=dAt, hping=hping, a_ready=a_ready)
    yg_all = (qyg, qyg.count)

    qys = b.fresh_qsem()
    qo1 = b.fresh_qsem()
    o1_marks = {}
    for c in range(ntc):
        t0 = c * NMAX

        @sec(b, "sync")
        def _(s, c=c, t0=t0):
            s.wait_ge(yg_all[0].h, yg_all[1])
            if c > 0:
                s.wait_ge(mm.h, mm.count)
            for dt_i in range(16):
                qys.inc(s.dma_start(out=uslc[dt_i][:, :],
                                    in_=ygdram[dt_i * P:(dt_i + 1) * P, t0:t0 + NMAX]), 16)
        ys_mark = qys.count
        d4 = linear_stream_stage(b, com, src_tiles=uslc, wdram_bf=wbf["W_out"],
                                 d_in=2048, d_out=1024, pool=ctx.pool, t0=t0,
                                 tc=NMAX, mm_sem=mm, done_sem=sc, qp=None,
                                 panels=panels, pre_waits=[(qys, ys_mark)],
                                 src_t0=0, kw_dma_pre=[wt_pre])
        for ot, bank_i, ps, mark, t0_ in d4:
            @sec(b, "scalar")
            def _(s, ot=ot, ps=ps, mark=mark):
                s.wait_ge(mm.h, mark)
                prev = o1_marks.get(ot % 4)
                if prev is not None:
                    s.wait_ge(qo1.h, prev)
                sc.inc(s.copy(out=xz_sb[ot % 4][:, :], in_=ps))
            ctx.pool.release(bank_i, sc)
            sc_m = sc.count

            @sec(b, "sync")
            def _(s, ot=ot, t0=t0, sc_m=sc_m):
                s.wait_ge(sc.h, sc_m)
                qo1.inc(s.dma_start(out=udram[ot * P:(ot + 1) * P, t0:t0 + NMAX],
                                    in_=xz_sb[ot % 4][:, :]), 16)
            o1_marks[ot % 4] = qo1.count
    o1_done = (qo1, qo1.count)

    qo2 = b.fresh_qsem()
    for c in range(ntc):
        t0 = c * NMAX

        @sec(b, "sync")
        def _(s, c=c, t0=t0):
            s.wait_ge(qo1.h, o1_done[1])
            if c > 0:
                s.wait_ge(mm.h, mm.count)
            for k in range(8):
                qo2.inc(s.dma_start(out=uslc[k][:, :],
                                    in_=udram[k * P:(k + 1) * P, t0:t0 + NMAX]), 16)
        o2_mark = qo2.count
        rc = ResidChunk(b, com, ctx, xdram, c, pre_waits=[(qo2, o2_mark)])
        d5 = linear_stream_stage(b, com, src_tiles=uslc[0:8], wdram_bf=wbf["fc2"],
                                 d_in=1024, d_out=1024, pool=ctx.pool, t0=t0, tc=NMAX,
                                 mm_sem=mm, done_sem=ev, qp=None, panels=panels,
                                 pre_waits=[(qo2, o2_mark)], src_t0=0,
                                 kw_dma_pre=[wt_pre])

        def ep5(e, ot, ps, _t0, rc=rc):
            e.tensor_tensor(out=rc.tile(ot)[:, :], in0=ps, in1=rc.tile(ot)[:, :],
                            op=ALU.add)
            return e.tensor_scalar_add(out=rc.tile(ot)[:, :], in0=rc.tile(ot)[:, :],
                                       scalar1=fc2b.pp(ot))
        run_epilogues(b, "vector", d5, mm, ctx.pool, ev, ep5,
                      extra_waits=[rc.dma_mark])
        final = rc.writeback(ev, ev.count)
    return final



def l2ffn_router(b, com, ctx, *, GW, lbr, ap_marks, mask_fm, tag):
    T, ntc = ctx.T, ctx.ntc
    sc = b.sem("g_sc")
    ev = b.sem("g_ev")
    mm = b.sem("g_mm")
    lg = b.sbuf(f"{tag}_lg", [P, 8], F32)
    msk = b.sbuf(f"{tag}_msk", [P, 8 * 4], F32)
    mrow = b.sbuf(f"{tag}_mr", [P, 4], F32)
    selbf = b.sbuf(f"{tag}_selbf", [P, 16], BF16)
    lsum = b.sbuf(f"{tag}_ls", [1, 16], F32)

    sums_bank = {}
    for tt in range(T // P):
        stt = {}

        @sec(b, "tensor")
        def _(t, tt=tt):
            if tt == 0:
                t.wait_ge(ev.h, ap_marks[-1])
            else:
                t.wait_ge(ev.h, ev.count)
            stt["bi"], stt["bank"] = ctx.aux.acquire(t)
            for k in range(8):
                t.matmul(stt["bank"][:, 0:8], ctx.xl.tiles[k][:, tt * P:(tt + 1) * P],
                         GW.tiles[k][:, :], start=(k == 0), stop=False)
            mm.inc(t.matmul(stt["bank"][:, 0:8], com.ones_row_f[0:1, :], lbr[0:1, :],
                            start=False, stop=True))
        lg_mark = mm.count

        @sec(b, "scalar")
        def _(s, tt=tt):
            s.wait_ge(mm.h, lg_mark)
            sc.inc(s.activation(out=lg[:, :], in_=stt["bank"][:, 0:8], func=AF.Sigmoid))
        sg_mark = sc.count

        @sec(b, "vector")
        def _(v, tt=tt):
            v.wait_ge(sc.h, sg_mark)
            ctx.aux.release(stt["bi"], sc)
            M = lambda j: msk[:, j * 8:(j + 1) * 8]
            v.reduce_max(out=mrow[:, 0:1], in_=lg[:, :], axis=AX.X)
            com.gap(v)
            v.tensor_single_scalar(out=M(0), in_=lg[:, :], scalar=mrow[:, 0:1], op=ALU.is_ge)
            com.gap(v)
            v.scalar_tensor_tensor(out=M(1), in0=M(0), scalar=-8192.0, in1=lg[:, :],
                                   op0=ALU.mult, op1=ALU.add)
            com.gap(v)
            v.reduce_max(out=mrow[:, 1:2], in_=M(1), axis=AX.X)
            com.gap(v)
            v.tensor_single_scalar(out=M(1), in_=M(1), scalar=mrow[:, 1:2], op=ALU.is_ge)
            v.tensor_tensor(out=mrow[:, 2:3], in0=mrow[:, 0:1], in1=mrow[:, 1:2], op=ALU.add)
            com.gap(v)
            v.tensor_scalar_add(out=mrow[:, 2:3], in0=mrow[:, 2:3], scalar1=1e-8)
            com.gap(v)
            v.reciprocal(out=mrow[:, 3:4], in_=mrow[:, 2:3])
            com.gap(v)
            v.tensor_tensor(out=mrow[:, 0:1], in0=mrow[:, 0:1], in1=mrow[:, 3:4], op=ALU.mult)
            v.tensor_tensor(out=mrow[:, 1:2], in0=mrow[:, 1:2], in1=mrow[:, 3:4], op=ALU.mult)
            v.tensor_single_scalar(out=M(2), in_=M(0), scalar=mrow[:, 0:1], op=ALU.mult)
            v.tensor_tensor(out=M(3), in0=M(0), in1=M(1), op=ALU.add)
            v.scalar_tensor_tensor(out=M(2), in0=M(1), scalar=mrow[:, 1:2], in1=M(2),
                                   op0=ALU.mult, op1=ALU.add)
            com.gap(v)
            ev.inc(v.tensor_copy(out=selbf[:, 0:16], in_=msk[:, 16:32]))
        tp_mark = ev.count

        @sec(b, "tensor")
        def _(t, tt=tt):
            t.wait_ge(ev.h, tp_mark)
            stt["bj"], stt["bank2"] = ctx.aux2.acquire(t)
            stt["bk"], stt["bank3"] = ctx.aux2.acquire(t)
            t.transpose(stt["bank2"].bitcast(BF16)[0:8, 0:P], selbf[:, 0:8],
                        com.ident_bf[:, :])
            t.transpose(stt["bank3"].bitcast(BF16)[0:8, 0:P], selbf[:, 8:16],
                        com.ident_bf[:, :])
            if tt == 0:
                sums_bank["i"], sums_bank["b"] = ctx.pool.acquire(t)
            t.matmul(sums_bank["b"][0:1, 0:8], com.ones_col[:, :], selbf[:, 0:8],
                     start=(tt == 0), stop=False)
            ins = t.matmul(sums_bank["b"][0:1, 8:16], com.ones_col[:, :], selbf[:, 8:16],
                           start=False, stop=(tt == T // P - 1))
            mm.inc(ins)
        tpd_mark = mm.count

        @sec(b, "vector")
        def _(v, tt=tt):
            v.wait_ge(mm.h, tpd_mark)
            v.tensor_copy(out=mask_fm[0][0:8, tt * P:(tt + 1) * P],
                          in_=stt["bank2"].bitcast(BF16)[0:8, 0:P])
            ins = v.tensor_copy(out=mask_fm[1][0:8, tt * P:(tt + 1) * P],
                                in_=stt["bank3"].bitcast(BF16)[0:8, 0:P])
            ev.inc(ins)
            ctx.aux2.release(stt["bj"], ev)
            ctx.aux2.release(stt["bk"], ev)

    @sec(b, "vector")
    def _(v):
        v.wait_ge(mm.h, mm.count)
        ev.inc(v.tensor_copy(out=lsum[0:1, 0:16], in_=sums_bank["b"][0:1, 0:16]))
        ctx.pool.release(sums_bank["i"], ev)
    return lsum, (ev, ev.count)



def block_l2ffn(b, com, ctx, *, xdram, wd, wbf, mode, stats_out=None, gstats=None,
                tag=None, guards=()):
    tag = tag or f"l2f{b.uid()}"
    T, ntc, nt = ctx.T, ctx.ntc, ctx.nt
    qw = b.fresh_qsem()
    qpre = b.fresh_qsem()
    sc = b.sem("g_sc")
    ev = b.sem("g_ev")
    mm = b.sem("g_mm")

    GW = WRes(b, f"{tag}_gw", 1024, 8)
    lbr = b.sbuf(f"{tag}_lb", [1, 8], F32)

    @sec(b, "gpsimd")
    def _(g):
        GW.load(g, wd["gwT"], qw)
        qw.inc(g.dma_start(out=lbr[:, :], in_=wd["lb_row"][:, :]), 16)
    wt_cols = qw.count

    ap_marks = ln_fm(b, com, ctx, xdram=xdram, guards=list(guards) + [(qw, wt_cols)])
    mask_fm = [b.sbuf(f"{tag}_rwfm", [8, T], BF16), b.sbuf(f"{tag}_selfm", [8, T], BF16)]
    lsum, lsum_rdy = l2ffn_router(b, com, ctx, GW=GW, lbr=lbr, ap_marks=ap_marks,
                                  mask_fm=mask_fm, tag=tag)

    if mode == "stats":
        qso = b.fresh_qsem()

        @sec(b, "gpsimd")
        def _(g):
            g.wait_ge(lsum_rdy[0].h, lsum_rdy[1])
            qso.inc(g.dma_start(out=stats_out[:, :], in_=lsum[0:1, :]), 16)
        return [(qso, qso.count)]

    precast_bf16(b, qpre, [(wbf["eg"], wd["eg_T"], 8192, 512),
                           (wbf["eu"], wd["eu_T"], 8192, 512),
                           (wbf["ed"], wd["ed_T"], 4096, 1024),
                           (wbf["sg"], wd["sg_T"], 1024, 1024),
                           (wbf["su"], wd["su_T"], 1024, 1024),
                           (wbf["sd"], wd["sd_T"], 1024, 1024)])
    wt_pre = (qpre, qpre.count)
    gst = b.sbuf(f"{tag}_gst", [8, 16], F32)
    wmt = b.sbuf(f"{tag}_wmt", [P, 8], F32)
    qgs = b.fresh_qsem()

    @sec(b, "gpsimd")
    def _(g):
        qgs.inc(g.dma_start(out=gst[:, :], in_=gstats[:, :]), 16)
    stw = {}

    @sec(b, "tensor")
    def _(t):
        t.wait_ge(qgs.h, qgs.count)
        stw["bi"], stw["bank"] = ctx.aux.acquire(t)
        mm.inc(t.matmul(stw["bank"][0:1, 0:16], com.ones_col_f[0:8, :],
                        gst[:, :], start=True, stop=True))
    gsum_mark = mm.count

    @sec(b, "vector")
    def _(v):
        v.wait_ge(mm.h, gsum_mark)
        v.tensor_scalar_max(out=lsum[0:1, 8:16], in0=stw["bank"][0:1, 8:16], scalar1=1.0)
        com.gap(v)
        v.reciprocal(out=lsum[0:1, 8:16], in_=lsum[0:1, 8:16])
        com.gap(v)
        ev.inc(v.tensor_tensor(out=lsum[0:1, 0:8], in0=stw["bank"][0:1, 0:8],
                               in1=lsum[0:1, 8:16], op=ALU.mult))
        ctx.aux.release(stw["bi"], ev)
    wm_row = ev.count

    @sec(b, "tensor")
    def _(t):
        t.wait_ge(ev.h, wm_row)
        stw["bj"], stw["bank2"] = ctx.aux.acquire(t)
        mm.inc(t.matmul(stw["bank2"][:, 0:8], com.ones_row_f[0:1, :], lsum[0:1, 0:8],
                        start=True, stop=True))
    wmb_mark = mm.count

    @sec(b, "vector")
    def _(v):
        v.wait_ge(mm.h, wmb_mark)
        ev.inc(v.tensor_copy(out=wmt[:, :], in_=stw["bank2"][:, 0:8]))
        ctx.aux.release(stw["bj"], ev)
    wm_ready = (ev, ev.count)

    panels = [b.sbuf(f"{tag}_pan{i}", [P, 4 * P], BF16) for i in range(2)]
    gsb = [b.sbuf(f"{tag}_g{i}", [P, NMAX], BF16) for i in range(8)]
    hsb = [b.sbuf(f"{tag}_h{i}", [P, NMAX], BF16) for i in range(8)]
    acc = [b.sbuf(f"{tag}_acc{i}", [P, NMAX], F32) for i in range(8)]

    final = None
    for c in range(ntc):
        t0 = c * NMAX

        @sec(b, "vector")
        def _(v):
            v.wait_ge(ev.h, ev.count)
            for i in range(8):
                ins = v.memset(acc[i][:, :], 0.0)
            ev.inc(ins)
        accz = ev.count

        for e in range(8):
            stb = {}

            @sec(b, "tensor")
            def _(t, e=e, c=c, t0=t0):
                if c == 0 and e == 0:
                    t.wait_ge(wm_ready[0].h, wm_ready[1])
                stb["bi"], stb["bank"] = ctx.aux2.acquire(t)
                mm.inc(t.matmul(stb["bank"][:, 0:NMAX], com.sel16[0:8, e * P:(e + 1) * P],
                                mask_fm[1][0:8, t0:t0 + NMAX], start=True, stop=True))
            sbc_mark = mm.count
            dg = linear_stream_stage(b, com, src_tiles=ctx.xl.tiles,
                                     wdram_bf=wbf["eg"][e * 1024:(e + 1) * 1024, :],
                                     d_in=1024, d_out=512, pool=ctx.pool, t0=t0, tc=NMAX,
                                     mm_sem=mm, done_sem=sc, qp=None, panels=panels,
                                     pre_waits=[(ev, accz), wt_pre], src_t0=t0,
                                     kw_dma_pre=[wt_pre])
            run_epilogues(b, "scalar", dg, mm, ctx.pool, sc,
                          lambda e_, ot, ps, _t0: e_.activation(out=gsb[ot][:, :], in_=ps,
                                                                func=AF.Silu))
            du = linear_stream_stage(b, com, src_tiles=ctx.xl.tiles,
                                     wdram_bf=wbf["eu"][e * 1024:(e + 1) * 1024, :],
                                     d_in=1024, d_out=512, pool=ctx.pool, t0=t0, tc=NMAX,
                                     mm_sem=mm, done_sem=ev, qp=None, panels=panels,
                                     pre_waits=[(sc, sc.count)], src_t0=t0,
                                     kw_dma_pre=[wt_pre])

            def epu(v_, ot, ps, _t0, e=e):
                v_.tensor_tensor(out=hsb[ot][:, :], in0=ps, in1=gsb[ot][:, :], op=ALU.mult)
                return v_.scalar_tensor_tensor(out=hsb[ot][:, :], in0=stb["bank"][:, 0:NMAX],
                                               scalar=wmt[:, e:e + 1], in1=hsb[ot][:, :],
                                               op0=ALU.mult, op1=ALU.mult)
            run_epilogues(b, "vector", du, mm, ctx.pool, ev, epu,
                          extra_waits=[(mm, sbc_mark)])
            ctx.aux2.release(stb["bi"], ev)
            dd = linear_stream_stage(b, com, src_tiles=hsb[0:4], wdram_bf=wbf["ed"][e * 512:(e + 1) * 512, :],
                                     d_in=512, d_out=1024, pool=ctx.pool, t0=t0, tc=NMAX,
                                     mm_sem=mm, done_sem=ev, qp=None, panels=panels,
                                     pre_waits=[(ev, ev.count)], src_t0=0,
                                     kw_dma_pre=[wt_pre])
            run_epilogues(b, "vector", dd, mm, ctx.pool, ev,
                          lambda v_, ot, ps, _t0: v_.tensor_tensor(out=acc[ot][:, :], in0=ps,
                                                                   in1=acc[ot][:, :], op=ALU.add))
        dg = linear_stream_stage(b, com, src_tiles=ctx.xl.tiles, wdram_bf=wbf["sg"],
                                 d_in=1024, d_out=1024, pool=ctx.pool, t0=t0, tc=NMAX,
                                 mm_sem=mm, done_sem=sc, qp=None, panels=panels,
                                 pre_waits=[(ev, ev.count)], src_t0=t0, kw_dma_pre=[wt_pre])
        run_epilogues(b, "scalar", dg, mm, ctx.pool, sc,
                      lambda e_, ot, ps, _t0: e_.activation(out=gsb[ot][:, :], in_=ps,
                                                            func=AF.Silu))
        du = linear_stream_stage(b, com, src_tiles=ctx.xl.tiles, wdram_bf=wbf["su"],
                                 d_in=1024, d_out=1024, pool=ctx.pool, t0=t0, tc=NMAX,
                                 mm_sem=mm, done_sem=ev, qp=None, panels=panels,
                                 pre_waits=[(sc, sc.count)], src_t0=t0, kw_dma_pre=[wt_pre])
        run_epilogues(b, "vector", du, mm, ctx.pool, ev,
                      lambda v_, ot, ps, _t0: v_.tensor_tensor(out=hsb[ot][:, :], in0=ps,
                                                               in1=gsb[ot][:, :], op=ALU.mult))
        rc = ResidChunk(b, com, ctx, xdram, c, pre_waits=[(ev, ev.count)])
        dd = linear_stream_stage(b, com, src_tiles=hsb, wdram_bf=wbf["sd"],
                                 d_in=1024, d_out=1024, pool=ctx.pool, t0=t0, tc=NMAX,
                                 mm_sem=mm, done_sem=ev, qp=None, panels=panels,
                                 pre_waits=[(ev, ev.count)], src_t0=0, kw_dma_pre=[wt_pre])

        def epf(v_, ot, ps, _t0, rc=rc):
            v_.tensor_tensor(out=acc[ot][:, :], in0=ps, in1=acc[ot][:, :], op=ALU.add)
            return v_.scalar_tensor_tensor(out=rc.tile(ot)[:, :], in0=acc[ot][:, :],
                                           scalar=com.const_pp(0.5), in1=rc.tile(ot)[:, :],
                                           op0=ALU.mult, op1=ALU.add)
        run_epilogues(b, "vector", dd, mm, ctx.pool, ev, epf,
                      extra_waits=[rc.dma_mark])
        final = rc.writeback(ev, ev.count)
    return final



def block_l3ffn(b, com, ctx, *, xdram, wd, wbf, mode, stats_out=None, gcnts=None,
                maskrow=None, tag=None, guards=()):
    tag = tag or f"l3f{b.uid()}"
    T, ntc, nt = ctx.T, ctx.ntc, ctx.nt
    CAP = 8192.0
    qw = b.fresh_qsem()
    qpre = b.fresh_qsem()
    sc = b.sem("g_sc")
    ev = b.sem("g_ev")
    mm = b.sem("g_mm")

    WR = WRes(b, f"{tag}_wr", 1024, 8)
    brr = b.sbuf(f"{tag}_br", [1, 8], F32)

    @sec(b, "gpsimd")
    def _(g):
        WR.load(g, wd["WrT"], qw)
        qw.inc(g.dma_start(out=brr[:, :], in_=wd["br_row"][:, :]), 16)
    wt_cols = qw.count

    ap_marks = ln_fm(b, com, ctx, xdram=xdram, guards=list(guards) + [(qw, wt_cols)])

    lg = b.sbuf(f"{tag}_lg", [P, 8], F32)
    msk = b.sbuf(f"{tag}_msk", [P, 8 * 4], F32)
    mrow = b.sbuf(f"{tag}_mr", [P, 4], F32)
    selbf = b.sbuf(f"{tag}_selbf", [P, 16], BF16)
    rw_fm = b.sbuf(f"{tag}_rwfm", [8, T], BF16)
    sel_fm = ctx.bigscr[0:8, 2 * T:3 * T]
    rank_fm = ctx.bigscr[:, :].bitcast(F32)[0:8, 0:T]
    keep_fm = sel_fm

    for tt in range(T // P):
        stt = {}

        @sec(b, "tensor")
        def _(t, tt=tt):
            if tt == 0:
                t.wait_ge(ev.h, ap_marks[-1])
            else:
                t.wait_ge(ev.h, ev.count)
            stt["bi"], stt["bank"] = ctx.aux.acquire(t)
            for k in range(8):
                t.matmul(stt["bank"][:, 0:8], ctx.xl.tiles[k][:, tt * P:(tt + 1) * P],
                         WR.tiles[k][:, :], start=(k == 0), stop=False)
            mm.inc(t.matmul(stt["bank"][:, 0:8], com.ones_row_f[0:1, :], brr[0:1, :],
                            start=False, stop=True))
        lg_mark = mm.count

        @sec(b, "vector")
        def _(v, tt=tt):
            v.wait_ge(mm.h, lg_mark)
            M = lambda j: msk[:, j * 8:(j + 1) * 8]
            v.tensor_copy(out=lg[:, :], in_=stt["bank"][:, 0:8])
            ctx.aux.release(stt["bi"], ev)
            com.gap(v)
            v.reduce_max(out=mrow[:, 0:1], in_=lg[:, :], axis=AX.X)
            com.gap(v)
            v.tensor_single_scalar(out=M(0), in_=lg[:, :], scalar=mrow[:, 0:1], op=ALU.is_ge)
            com.gap(v)
            v.scalar_tensor_tensor(out=M(1), in0=M(0), scalar=-8192.0, in1=lg[:, :],
                                   op0=ALU.mult, op1=ALU.add)
            com.gap(v)
            v.reduce_max(out=mrow[:, 1:2], in_=M(1), axis=AX.X)
            com.gap(v)
            v.tensor_single_scalar(out=M(1), in_=M(1), scalar=mrow[:, 1:2], op=ALU.is_ge)
            ev.inc(v.tensor_tensor(out=mrow[:, 2:3], in0=mrow[:, 0:1], in1=mrow[:, 1:2],
                                   op=ALU.subtract))
        d_mark = ev.count

        @sec(b, "scalar")
        def _(s, tt=tt):
            s.wait_ge(ev.h, d_mark)
            sc.inc(s.activation(out=mrow[:, 2:3], in_=mrow[:, 2:3], func=AF.Sigmoid))
        g_mark = sc.count

        @sec(b, "vector")
        def _(v, tt=tt):
            v.wait_ge(sc.h, g_mark)
            M = lambda j: msk[:, j * 8:(j + 1) * 8]
            v.tensor_scalar(out=mrow[:, 3:4], in0=mrow[:, 2:3], scalar1=-1.0, scalar2=1.0,
                            op0=ALU.mult, op1=ALU.add)
            v.tensor_single_scalar(out=M(2), in_=M(0), scalar=mrow[:, 2:3], op=ALU.mult)
            v.tensor_tensor(out=M(3), in0=M(0), in1=M(1), op=ALU.add)
            v.scalar_tensor_tensor(out=M(2), in0=M(1), scalar=mrow[:, 3:4], in1=M(2),
                                   op0=ALU.mult, op1=ALU.add)
            com.gap(v)
            ev.inc(v.tensor_copy(out=selbf[:, 0:16], in_=msk[:, 16:32]))
        tp_mark = ev.count

        @sec(b, "tensor")
        def _(t, tt=tt):
            t.wait_ge(ev.h, tp_mark)
            stt["bj"], stt["bank2"] = ctx.aux2.acquire(t)
            stt["bk"], stt["bank3"] = ctx.aux2.acquire(t)
            t.transpose(stt["bank2"].bitcast(BF16)[0:8, 0:P], selbf[:, 0:8],
                        com.ident_bf[:, :])
            mm.inc(t.transpose(stt["bank3"].bitcast(BF16)[0:8, 0:P], selbf[:, 8:16],
                               com.ident_bf[:, :]))
        tpd_mark = mm.count

        @sec(b, "vector")
        def _(v, tt=tt):
            v.wait_ge(mm.h, tpd_mark)
            v.tensor_copy(out=rw_fm[0:8, tt * P:(tt + 1) * P],
                          in_=stt["bank2"].bitcast(BF16)[0:8, 0:P])
            ins = v.tensor_copy(out=sel_fm[0:8, tt * P:(tt + 1) * P],
                                in_=stt["bank3"].bitcast(BF16)[0:8, 0:P])
            ev.inc(ins)
            ctx.aux2.release(stt["bj"], ev)
            ctx.aux2.release(stt["bk"], ev)
    router_done = (ev, ev.count)

    ones8 = ctx.bigscr[0:8, 3 * T:4 * T]

    @sec(b, "vector")
    def _(v):
        v.wait_ge(router_done[0].h, router_done[1])
        v.memset(ones8[0:8, :], 1.0)
        v.tensor_tensor_scan(out=rank_fm[0:8, :], data0=ones8[0:8, :], data1=sel_fm[0:8, :],
                             initial=0.0, op0=ALU.mult, op1=ALU.add)
        ev.inc(v.tensor_tensor(out=rank_fm[0:8, :], in0=rank_fm[0:8, :],
                               in1=sel_fm[0:8, :], op=ALU.subtract))
    rank_mark = ev.count

    if mode == "stats":
        cnts = b.sbuf(f"{tag}_cnt", [8, 1], F32)

        @sec(b, "vector")
        def _(v):
            v.wait_ge(ev.h, rank_mark)
            ev.inc(v.tensor_tensor(out=cnts[0:8, 0:1], in0=rank_fm[0:8, T - 1:T],
                                   in1=sel_fm[0:8, T - 1:T], op=ALU.add))
        qso = b.fresh_qsem()

        @sec(b, "gpsimd")
        def _(g):
            g.wait_ge(ev.h, ev.count)
            qso.inc(g.dma_start(out=stats_out[:, :], in_=cnts[0:8, :]), 16)
        return [(qso, qso.count)]

    precast_bf16(b, qpre, [(wbf["W1"], wd["W1_T"], 8192, 4096),
                           (wbf["W2"], wd["W2_T"], 32768, 1024)])
    wt_pre = (qpre, qpre.count)
    b1c = Cols(b, f"{tag}_b1c", 256 * P)
    b2c = Cols(b, f"{tag}_b2c", 64 * P)
    gct = b.sbuf(f"{tag}_gct", [8, 8], F32)
    mrt = b.sbuf(f"{tag}_mrt", [8, 1], F32)
    offr = b.sbuf(f"{tag}_offr", [1, 8], F32)
    offc = b.sbuf(f"{tag}_offc", [8, 1], F32)
    qgs = b.fresh_qsem()

    @sec(b, "gpsimd")
    def _(g):
        b1c.load(g, wd["b1_cols"], qw)
        b2c.load(g, wd["b2_cols"], qw)
        qgs.inc(g.dma_start(out=gct[:, :], in_=gcnts[:, :]), 16)
        qgs.inc(g.dma_start(out=mrt[:, :], in_=maskrow[:, :]), 16)
    colb_mark = qw.count
    sto = {}

    @sec(b, "tensor")
    def _(t):
        t.wait_ge(qgs.h, qgs.count)
        sto["bi"], sto["bank"] = ctx.aux.acquire(t)
        mm.inc(t.matmul(sto["bank"][0:1, 0:8], mrt[0:8, :], gct[:, :],
                        start=True, stop=True))
    off_mark = mm.count

    @sec(b, "vector")
    def _(v):
        v.wait_ge(mm.h, off_mark)
        ev.inc(v.tensor_copy(out=offr[0:1, :], in_=sto["bank"][0:1, 0:8]))
        ctx.aux.release(sto["bi"], ev)
    offr_mark = ev.count

    @sec(b, "tensor")
    def _(t):
        t.wait_ge(ev.h, offr_mark)
        sto["bj"], sto["bank2"] = ctx.aux.acquire(t)
        mm.inc(t.matmul(sto["bank2"][0:8, 0:1], offr[0:1, :], com.ones_row_f[0:1, 0:1],
                        start=True, stop=True))
    offc_mark = mm.count

    @sec(b, "vector")
    def _(v):
        v.wait_ge(mm.h, offc_mark)
        v.wait_ge(ev.h, rank_mark)
        v.tensor_copy(out=offc[0:8, :], in_=sto["bank2"][0:8, 0:1])
        ctx.aux.release(sto["bj"], ev)
        com.gap(v)
        v.tensor_scalar_add(out=rank_fm[0:8, :], in0=rank_fm[0:8, :], scalar1=offc[0:8, 0:1])
        v.tensor_single_scalar(out=rank_fm[0:8, :], in_=rank_fm[0:8, :], scalar=CAP,
                               op=ALU.is_lt)
        v.tensor_tensor(out=keep_fm[0:8, :], in0=rank_fm[0:8, :], in1=sel_fm[0:8, :],
                        op=ALU.mult)
        ev.inc(v.tensor_tensor(out=rw_fm[0:8, :], in0=rw_fm[0:8, :], in1=keep_fm[0:8, :],
                               op=ALU.mult))
    w_ready = (ev, ev.count)

    panels = [b.sbuf(f"{tag}_pan{i}", [P, 4 * P], BF16) for i in range(2)]
    hb = [ctx.bigscr[:, i * NMAX:(i + 1) * NMAX] for i in range(32)]
    wsb = b.sbuf(f"{tag}_wsb", [P, NMAX], F32)
    acc = [b.sbuf(f"{tag}_acc{i}", [P, NMAX], F32) for i in range(8)]
    scr2 = b.sbuf(f"{tag}_scr2", [P, NMAX], F32)

    final = None
    for c in range(ntc):
        t0 = c * NMAX

        @sec(b, "vector")
        def _(v):
            v.wait_ge(ev.h, ev.count)
            for i in range(8):
                ins = v.memset(acc[i][:, :], 0.0)
            ev.inc(ins)
        accz = ev.count

        for e in range(8):
            stb = {}

            @sec(b, "tensor")
            def _(t, e=e, c=c, t0=t0):
                if c == 0 and e == 0:
                    t.wait_ge(w_ready[0].h, w_ready[1])
                stb["bi"], stb["bank"] = ctx.aux2.acquire(t)
                mm.inc(t.matmul(stb["bank"][:, 0:NMAX], com.sel16[0:8, e * P:(e + 1) * P],
                                rw_fm[0:8, t0:t0 + NMAX], start=True, stop=True))
            wbc_mark = mm.count

            @sec(b, "vector")
            def _(v, e=e):
                v.wait_ge(mm.h, wbc_mark)
                ev.inc(v.tensor_copy(out=wsb[:, :], in_=stb["bank"][:, 0:NMAX]))
                ctx.aux2.release(stb["bi"], ev)
            wsb_mark = ev.count

            dh = linear_stream_stage(b, com, src_tiles=ctx.xl.tiles,
                                     wdram_bf=wbf["W1"][e * 1024:(e + 1) * 1024, :],
                                     d_in=1024, d_out=4096, pool=ctx.pool, t0=t0, tc=NMAX,
                                     mm_sem=mm, done_sem=sc, qp=None, panels=panels,
                                     pre_waits=[(ev, accz), wt_pre, (qw, colb_mark)],
                                     src_t0=t0, kw_dma_pre=[wt_pre])
            run_epilogues(b, "scalar", dh, mm, ctx.pool, sc,
                          lambda e_, ot, ps, _t0, e=e: e_.activation(
                              out=hb[ot][:, :], in_=ps, func=AF.Gelu,
                              bias=b1c.pp(e * 32 + ot)))
            dy = linear_stream_stage(b, com, src_tiles=hb, wdram_bf=wbf["W2"][e * 4096:(e + 1) * 4096, :],
                                     d_in=4096, d_out=1024, pool=ctx.pool, t0=t0, tc=NMAX,
                                     mm_sem=mm, done_sem=ev, qp=None, panels=panels,
                                     pre_waits=[(sc, sc.count)], src_t0=0,
                                     kw_dma_pre=[wt_pre])

            def epy(v_, ot, ps, _t0, e=e):
                v_.tensor_scalar_add(out=scr2[:, :], in0=ps, scalar1=b2c.pp(e * 8 + ot))
                v_.tensor_tensor(out=scr2[:, :], in0=scr2[:, :], in1=wsb[:, :], op=ALU.mult)
                return v_.tensor_tensor(out=acc[ot][:, :], in0=acc[ot][:, :],
                                        in1=scr2[:, :], op=ALU.add)
            run_epilogues(b, "vector", dy, mm, ctx.pool, ev, epy,
                          extra_waits=[(ev, wsb_mark)])
        rc = ResidChunk(b, com, ctx, xdram, c, pre_waits=[(ev, ev.count)])

        @sec(b, "vector")
        def _(v, c=c):
            v.wait_ge(ev.h, ev.count)
            v.wait_ge(rc.dma_mark[0].h, rc.dma_mark[1])
            for ot in range(8):
                ins = v.scalar_tensor_tensor(out=rc.tile(ot)[:, :], in0=acc[ot][:, :],
                                             scalar=com.const_pp(0.5), in1=rc.tile(ot)[:, :],
                                             op0=ALU.mult, op1=ALU.add)
            ev.inc(ins)
        final = rc.writeback(ev, ev.count)
    return final



def block_l3attn(b, com, ctx, *, xdram, wd, wbf, qkvdram, gatedram, odram,
                 tag=None, guards=()):
    tag = tag or f"l3a{b.uid()}"
    T, ntc, nt = ctx.T, ctx.ntc, ctx.nt
    C = P
    qw = b.fresh_qsem()
    qpre = b.fresh_qsem()
    sc = b.sem("g_sc")
    ev = b.sem("g_ev")
    mm = b.sem("g_mm")

    cwq = Cols(b, f"{tag}_cwq", 32 * P)
    cwk = Cols(b, f"{tag}_cwk", 32 * P)
    cwv = Cols(b, f"{tag}_cwv", 32 * P)
    WB = WRes(b, f"{tag}_wb", 1024, 16)
    WA = WRes(b, f"{tag}_wa", 1024, 16)
    alog = b.sbuf(f"{tag}_alog", [16, 1], F32)
    dtbc = b.sbuf(f"{tag}_dtb", [16, 1], F32)
    nwc = b.sbuf(f"{tag}_nw", [64, 1], F32)

    @sec(b, "gpsimd")
    def _(g):
        cwq.load(g, wd["convq_cw"], qw)
        cwk.load(g, wd["convk_cw"], qw)
        cwv.load(g, wd["convv_cw"], qw)
        WB.load(g, wd["WbT"], qw)
        WA.load(g, wd["WaT"], qw)
        qw.inc(g.dma_start(out=alog[:, :], in_=wd["alog_col"][:, :]), 16)
        qw.inc(g.dma_start(out=dtbc[:, :], in_=wd["dtb_col"][:, :]), 16)
        qw.inc(g.dma_start(out=nwc[:, :], in_=wd["normw_col"][:, :]), 16)
    wt_cols = qw.count

    @sec(b, "scalar")
    def _(s):
        s.wait_ge(qw.h, wt_cols)
        sc.inc(s.activation(out=alog[:, :], in_=alog[:, :], func=AF.Exp))
    negexp_m = sc.count

    @sec(b, "vector")
    def _(v):
        v.wait_ge(sc.h, negexp_m)
        ev.inc(v.tensor_scalar_mul(out=alog[:, :], in0=alog[:, :], scalar1=-1.0))
    a_ready = (ev, ev.count)

    precast_bf16(b, qpre, [(wbf["Wq"], wd["WqT"], 1024, 1024),
                           (wbf["Wk"], wd["WkT"], 1024, 1024),
                           (wbf["Wv"], wd["WvT"], 1024, 1024),
                           (wbf["Wg"], wd["WgT"], 1024, 1024),
                           (wbf["Wo"], wd["WoT"], 1024, 1024)])
    wt_pre = (qpre, qpre.count)

    ap_marks = ln_fm(b, com, ctx, xdram=xdram, guards=list(guards) + [(qw, wt_cols)])

    panels = [b.sbuf(f"{tag}_pan{i}", [P, 4 * P], BF16) for i in range(2)]
    big32 = ctx.bigscr[:, :].bitcast(F32)
    convacc = big32[:, 0:T]
    proj_sb = ctx.bigscr[0:P, 2 * T:3 * T]
    qst = b.fresh_qsem()

    wslice = [b.sbuf(f"{tag}_ws{i}", [P, P], BF16) for i in range(8)]
    qwsl = b.fresh_qsem()
    nrow = b.sbuf(f"{tag}_nrow", [1, 2 * NMAX], F32)

    def projection(wkey, cw, dst_rows, do_conv, do_norm, qscale, dstdram):
        for ti in range(8):
            @sec(b, "gpsimd")
            def _(g, ti=ti):
                g.wait_ge(mm.h, mm.count)
                if ti == 0:
                    g.wait_ge(wt_pre[0].h, wt_pre[1])
                for k in range(8):
                    qwsl.inc(g.dma_start(out=wslice[k][:, :],
                                         in_=wbf[wkey][k * P:(k + 1) * P, ti * P:(ti + 1) * P]), 16)
            wsl_m = qwsl.count

            for c in range(ntc):
                t0 = c * NMAX
                stp = {}

                @sec(b, "tensor")
                def _(t, c=c, t0=t0, ti=ti):
                    if c == 0:
                        t.wait_ge(qwsl.h, wsl_m)
                        t.wait_ge(ev.h, ap_marks[-1])
                        t.wait_ge(ev.h, ev.count)
                    stp["bi"], stp["bank"] = ctx.pool.acquire(t)
                    for k in range(8):
                        ins = t.matmul(stp["bank"][:, 0:NMAX], wslice[k][:, :],
                                       ctx.xl.tiles[k][:, t0:t0 + NMAX],
                                       start=(k == 0), stop=(k == 7))
                    mm.inc(ins)
                mmk = mm.count

                @sec(b, "vector")
                def _(v, c=c, t0=t0):
                    v.wait_ge(mm.h, mmk)
                    if c == 0:
                        v.wait_ge(qst.h, qst.count)
                    ev.inc(v.tensor_copy(out=proj_sb[:, t0:t0 + NMAX], in_=stp["bank"][:, 0:NMAX]))
                ctx.pool.release(stp["bi"], ev)
            tile_m = ev.count

            if do_conv:
                @sec(b, "vector")
                def _(v, ti=ti):
                    v.wait_ge(ev.h, tile_m)
                    v.memset(convacc[:, 0:T], 0.0)
                    for w in range(4):
                        sh = 3 - w
                        ins = v.scalar_tensor_tensor(out=convacc[:, sh:T],
                                                     in0=proj_sb[:, 0:T - sh],
                                                     scalar=cw.pp(ti * 4 + w),
                                                     in1=convacc[:, sh:T],
                                                     op0=ALU.mult, op1=ALU.add)
                    ev.inc(ins)
                cv_m = ev.count

                @sec(b, "scalar")
                def _(s, ti=ti):
                    s.wait_ge(ev.h, cv_m)
                    sc.inc(s.activation(out=proj_sb[:, 0:T],
                                        in_=convacc[:, 0:T], func=AF.Silu))
                silu_m = (sc, sc.count)
            else:
                silu_m = (ev, tile_m)

            if do_norm:
                @sec(b, "scalar")
                def _(s, ti=ti):
                    s.wait_ge(silu_m[0].h, silu_m[1])
                    sc.inc(s.activation(out=convacc[:, 0:T].bitcast(BF16)[:, 0:T],
                                        in_=proj_sb[:, 0:T], func=AF.Square))
                sq_m = sc.count
                sqbf = convacc[:, 0:T].bitcast(BF16)[:, 0:T]
                for half in range(2):
                    for c in range(ntc):
                        t0 = c * NMAX
                        stn = {}

                        @sec(b, "tensor")
                        def _(t, half=half, c=c, t0=t0):
                            if half == 0 and c == 0:
                                t.wait_ge(sc.h, sq_m)
                            stn["bi"], stn["bank"] = ctx.aux.acquire(t)
                            mm.inc(t.matmul(stn["bank"][0:1, 0:NMAX],
                                            com.ones_col[0:64, :] if half == 0 else com.ones_col[64:128, :],
                                            sqbf[64 * half:64 * half + 64, t0:t0 + NMAX],
                                            start=True, stop=True))
                        ssq_m = mm.count

                        @sec(b, "vector")
                        def _(v, c=c, t0=t0):
                            v.wait_ge(mm.h, ssq_m)
                            ev.inc(v.tensor_scalar_add(out=nrow[0:1, 0:NMAX],
                                                       in0=stn["bank"][0:1, 0:NMAX],
                                                       scalar1=1e-6))
                            ctx.aux.release(stn["bi"], ev)
                        ne_m = ev.count

                        @sec(b, "scalar")
                        def _(s, c=c):
                            s.wait_ge(ev.h, ne_m)
                            s.activation(out=nrow[0:1, NMAX:2 * NMAX], in_=nrow[0:1, 0:NMAX],
                                         func=AF.Ln)
                            sc.inc(s.activation(out=nrow[0:1, NMAX:2 * NMAX],
                                                in_=nrow[0:1, NMAX:2 * NMAX], func=AF.Exp,
                                                scale=-0.5))
                        rst_m = sc.count
                        if qscale != 1.0:
                            @sec(b, "vector")
                            def _(v):
                                v.wait_ge(sc.h, rst_m)
                                ev.inc(v.tensor_scalar_mul(out=nrow[0:1, NMAX:2 * NMAX],
                                                           in0=nrow[0:1, NMAX:2 * NMAX],
                                                           scalar1=qscale))
                            rst_m2 = (ev, ev.count)
                        else:
                            rst_m2 = (sc, rst_m)
                        stb2 = {}

                        @sec(b, "tensor")
                        def _(t, half=half, c=c):
                            t.wait_ge(rst_m2[0].h, rst_m2[1])
                            stb2["bi"], stb2["bank"] = ctx.aux2.acquire(t)
                            mm.inc(t.matmul(stb2["bank"][0:64, 0:NMAX], com.ones_row_f[0:1, 0:64],
                                            nrow[0:1, NMAX:2 * NMAX], start=True, stop=True))
                        bcm = mm.count

                        @sec(b, "vector")
                        def _(v, half=half, c=c, t0=t0):
                            v.wait_ge(mm.h, bcm)
                            ins = v.tensor_tensor(out=proj_sb[64 * half:64 * half + 64, t0:t0 + NMAX],
                                                  in0=proj_sb[64 * half:64 * half + 64, t0:t0 + NMAX],
                                                  in1=stb2["bank"][0:64, 0:NMAX], op=ALU.mult)
                            ev.inc(ins)
                            ctx.aux2.release(stb2["bi"], ev)
            done_m = (ev, ev.count) if do_norm else silu_m

            @sec(b, "gpsimd")
            def _(g, ti=ti):
                g.wait_ge(done_m[0].h, done_m[1])
                qst.inc(g.dma_start(out=dstdram[(dst_rows + ti) * P:(dst_rows + ti + 1) * P, :],
                                    in_=proj_sb[:, 0:T]), 16)

    projection("Wq", cwq, 0, True, True, 0.125, qkvdram)
    projection("Wk", cwk, 8, True, True, 1.0, qkvdram)
    projection("Wv", cwv, 16, True, False, 1.0, qkvdram)
    projection("Wg", None, 0, False, False, 1.0, gatedram)
    proj_all = (qst, qst.count)
    if wd.get("dbg_snap") is not None:
        qsn = b.fresh_qsem()

        @sec(b, "sync")
        def _(s):
            s.wait_ge(proj_all[0].h, proj_all[1])
            for j in range(6):
                qsn.inc(s.dma_start(out=wd["dbg_snap"][j * P:(j + 1) * P, :],
                                    in_=qkvdram[(j * 4) * P:(j * 4 + 1) * P, :]), 16)
            s.wait_ge(qsn.h, qsn.count)

    ncg16 = (T // P) * 16
    bt_tm = b.sbuf(f"{tag}_bt", [P, ncg16], F32)
    gt_tm = b.sbuf(f"{tag}_gt", [P, ncg16], F32)
    Zs = b.sbuf(f"{tag}_zs", [16, NMAX], F32)
    for which in range(2):
        WW = WB if which == 0 else WA
        dst = bt_tm if which == 0 else gt_tm
        for c in range(ntc):
            t0 = c * NMAX
            stb = {}

            @sec(b, "tensor")
            def _(t, c=c, t0=t0, which=which, WW=WW):
                if which == 0 and c == 0:
                    t.wait_ge(ev.h, ap_marks[-1])
                    t.wait_ge(qw.h, wt_cols)
                stb["bi"], stb["bank"] = ctx.aux.acquire(t)
                for k in range(8):
                    ins = t.matmul(stb["bank"][0:16, 0:NMAX], WW.tiles[k][:, :],
                                   ctx.xl.tiles[k][:, t0:t0 + NMAX],
                                   start=(k == 0), stop=(k == 7))
                mm.inc(ins)
            pm = mm.count

            if which == 0:
                @sec(b, "scalar")
                def _(s):
                    s.wait_ge(mm.h, pm)
                    sc.inc(s.activation(out=Zs[:, :], in_=stb["bank"][0:16, 0:NMAX],
                                        func=AF.Sigmoid))
                zs_m = (sc, sc.count)
            else:
                @sec(b, "vector")
                def _(v):
                    v.wait_ge(mm.h, pm)
                    v.tensor_single_scalar(out=Zs[:, :], in_=stb["bank"][0:16, 0:NMAX],
                                           scalar=dtbc[0:16, 0:1], op=ALU.add)
                    ev.inc(v.tensor_relu(out=ctx.scr[0:16, 0:NMAX], in_=Zs[:, :]))
                r_m = ev.count

                @sec(b, "scalar")
                def _(s):
                    s.wait_ge(ev.h, r_m)
                    s.activation(out=Zs[:, :], in_=Zs[:, :], func=AF.Abs)
                    s.activation(out=Zs[:, :], in_=Zs[:, :], func=AF.Exp, scale=-1.0)
                    sc.inc(s.activation(out=Zs[:, :], in_=Zs[:, :], func=AF.Ln,
                                        bias=com.const_pp(1.0)[0:16, :]))
                l_m = sc.count

                @sec(b, "vector")
                def _(v):
                    v.wait_ge(sc.h, l_m)
                    v.tensor_tensor(out=Zs[:, :], in0=Zs[:, :],
                                    in1=ctx.scr[0:16, 0:NMAX], op=ALU.add)
                    ev.inc(v.tensor_single_scalar(out=Zs[:, :], in_=Zs[:, :],
                                                  scalar=alog[0:16, 0:1], op=ALU.mult))
                zs_m = (ev, ev.count)
            ctx.aux.release(stb["bi"], zs_m[0])

            for tt in range(NMAX // P):
                stt = {}

                @sec(b, "tensor")
                def _(t, tt=tt, zs_m=zs_m):
                    if tt == 0:
                        t.wait_ge(zs_m[0].h, zs_m[1])
                    stt["bi"], stt["bank"] = ctx.aux2.acquire(t)
                    mm.inc(t.transpose(stt["bank"][0:P, 0:16],
                                       Zs[0:16, tt * P:(tt + 1) * P],
                                       com.ident_f[0:16, 0:16]))
                tp_m = mm.count

                @sec(b, "vector")
                def _(v, c=c, tt=tt, dst=dst):
                    v.wait_ge(mm.h, tp_m)
                    cg = c * (NMAX // P) + tt
                    ev.inc(v.tensor_copy(out=dst[:, cg * 16:(cg + 1) * 16],
                                         in_=stt["bank"][0:P, 0:16]))
                    ctx.aux2.release(stt["bi"], ev)
    bg_tm_done = (ev, ev.count)


    nch16 = (T // P) * 16
    Gt_all = b.sbuf(f"{tag}_Gt", [P, nch16], F32)
    W_all = b.sbuf(f"{tag}_Wt", [P, nch16], F32)
    Gam_all = b.sbuf(f"{tag}_Gm", [P, nch16], F32)
    GamC_all = b.sbuf(f"{tag}_GmC", [P, nch16], F32)
    for c in range(T // P):
        c16 = c * 16
        sg = {}

        @sec(b, "tensor")
        def _(t, c=c, c16=c16):
            if c == 0:
                t.wait_ge(ev.h, bg_tm_done[1])
            sg["bi"], sg["bank"] = ctx.aux.acquire(t)
            t.matmul(sg["bank"][:, 0:16], com.m_up_incl_f[:, :],
                     gt_tm[:, c16:c16 + 16], start=True, stop=True)
            t.matmul(sg["bank"][:, 16:32], com.m_up_incl_f[:, :],
                     gt_tm[:, c16:c16 + 16], start=True, stop=False)
            t.matmul(sg["bank"][:, 16:32], com.m_lo_strict_f[:, :],
                     gt_tm[:, c16:c16 + 16], start=False, stop=True)
            mm.inc(t.matmul(sg["bank"][:, 32:48], com.m_lo_strict_f[:, :],
                            gt_tm[:, c16:c16 + 16], start=True, stop=True))
        gp_m = mm.count

        @sec(b, "vector")
        def _(v, c16=c16):
            v.wait_ge(mm.h, gp_m)
            v.tensor_copy(out=Gt_all[:, c16:c16 + 16], in_=sg["bank"][:, 0:16])
            v.tensor_copy(out=W_all[:, c16:c16 + 16], in_=sg["bank"][:, 32:48])
            ev.inc(v.tensor_copy(out=GamC_all[:, c16:c16 + 16],
                                 in_=sg["bank"][:, 16:32]))
            ctx.aux.release(sg["bi"], ev)
    gfam_m = ev.count

    @sec(b, "scalar")
    def _(s):
        s.wait_ge(ev.h, gfam_m)
        s.activation(out=Gam_all[:, :], in_=Gt_all[:, :], func=AF.Exp)
        s.activation(out=W_all[:, :], in_=W_all[:, :], func=AF.Exp)
        sc.inc(s.activation(out=GamC_all[:, :], in_=GamC_all[:, :], func=AF.Exp))
    gfam_s = sc.count

    if wd.get("dbg_bg") is not None:
        qbg = b.fresh_qsem()

        @sec(b, "sync")
        def _(s):
            s.wait_ge(sc.h, gfam_s)
            qbg.inc(s.dma_start(out=wd["dbg_bg"][0:P, :], in_=bt_tm[:, :]), 16)
            qbg.inc(s.dma_start(out=wd["dbg_bg"][P:2 * P, :], in_=gt_tm[:, :]), 16)
            qbg.inc(s.dma_start(out=wd["dbg_bg"][2 * P:3 * P, :], in_=Gt_all[:, :]), 16)
            qbg.inc(s.dma_start(out=wd["dbg_bg"][3 * P:4 * P, :], in_=Gam_all[:, :]), 16)
            qbg.inc(s.dma_start(out=wd["dbg_bg"][4 * P:5 * P, :], in_=W_all[:, :]), 16)
            qbg.inc(s.dma_start(out=wd["dbg_bg"][5 * P:6 * P, :], in_=GamC_all[:, :]), 16)
            s.wait_ge(qbg.h, qbg.count)

    orow = b.sbuf(f"{tag}_orow", [1, P], F32)
    nwbc = b.sbuf(f"{tag}_nwbc", [P, 64], F32)
    sn = {}

    @sec(b, "tensor")
    def _(t):
        t.wait_ge(qw.h, wt_cols)
        sn["bi"], sn["bank"] = ctx.aux.acquire(t)
        mm.inc(t.transpose(sn["bank"][0:1, 0:64], nwc[:, :], com.ident_f[0:64, 0:64]))
    nw_m = mm.count

    @sec(b, "vector")
    def _(v):
        v.wait_ge(mm.h, nw_m)
        ev.inc(v.tensor_copy(out=orow[0:1, 0:64], in_=sn["bank"][0:1, 0:64]))
        ctx.aux.release(sn["bi"], ev)
    nw_m2 = ev.count

    @sec(b, "tensor")
    def _(t):
        t.wait_ge(ev.h, nw_m2)
        sn["bj"], sn["bank2"] = ctx.aux.acquire(t)
        mm.inc(t.matmul(sn["bank2"][:, 0:64], com.ones_row_f[0:1, :],
                        orow[0:1, 0:64], start=True, stop=True))
    nw_m3 = mm.count

    @sec(b, "vector")
    def _(v):
        v.wait_ge(mm.h, nw_m3)
        ev.inc(v.tensor_scalar_mul(out=nwbc[:, :], in0=sn["bank2"][:, 0:64],
                                   scalar1=8.0))
        ctx.aux.release(sn["bj"], ev)

    qh_sb = b.sbuf(f"{tag}_qh", [64, T], BF16)
    kh_sb = b.sbuf(f"{tag}_kh", [64, T], BF16)
    vh_sb = b.sbuf(f"{tag}_vh", [64, T], BF16)
    gh_sb = b.sbuf(f"{tag}_gh", [64, T], BF16)
    S_f = b.sbuf(f"{tag}_Sf", [64, 64], F32)
    S_bf = b.sbuf(f"{tag}_Sb", [64, 64], BF16)
    Dm = b.sbuf(f"{tag}_Dm", [P, 4 * P], F32)
    Ap = [b.sbuf(f"{tag}_Ap{i}", [P, P], BF16) for i in range(4)]
    ut = b.sbuf(f"{tag}_ut", [P, 2 * 64], F32)
    vtm = b.sbuf(f"{tag}_vtm", [P, 64], BF16)
    wtm = b.sbuf(f"{tag}_wtm", [P, 64], BF16)
    ktm = b.sbuf(f"{tag}_ktm", [P, 64], BF16)
    gtm = b.sbuf(f"{tag}_gtm", [P, 64], F32)
    otm = b.sbuf(f"{tag}_otm", [P, 64], F32)
    msc = b.sbuf(f"{tag}_msc", [P, 2], F32)
    o_fm_sb = b.sbuf(f"{tag}_ofm", [64, T], BF16)
    qhl = b.fresh_qsem()
    qow = b.fresh_qsem()

    for h in range(16):
        @sec(b, "sync")
        def _(s, h=h):
            s.wait_ge(proj_all[0].h, proj_all[1])
            if h > 0:
                s.wait_ge(mm.h, mm.count)
            qhl.inc(s.dma_start(out=qh_sb[:, :], in_=qkvdram[h * 64:(h + 1) * 64, :]), 16)
            qhl.inc(s.dma_start(out=kh_sb[:, :], in_=qkvdram[1024 + h * 64:1024 + (h + 1) * 64, :]), 16)
            qhl.inc(s.dma_start(out=vh_sb[:, :], in_=qkvdram[2048 + h * 64:2048 + (h + 1) * 64, :]), 16)
            qhl.inc(s.dma_start(out=gh_sb[:, :], in_=gatedram[h * 64:(h + 1) * 64, :]), 16)
        hl_m = qhl.count

        @sec(b, "vector")
        def _(v, h=h):
            v.wait_ge(qhl.h, hl_m)
            if h == 0:
                v.wait_ge(sc.h, gfam_s)
            else:
                v.wait_ge(qow.h, qow.count)
            v.memset(S_f[:, :], 0.0)
            ev.inc(v.memset(S_bf[:, :], 0.0))

        for c in range(T // P):
            tc0 = c * P
            col = c * 16 + h
            sv = {}

            @sec(b, "tensor")
            def _(t, col=col):
                t.wait_ge(ev.h, ev.count)
                sv["b1"], sv["k1"] = ctx.aux.acquire(t)
                mm.inc(t.transpose(sv["k1"][0:1, 0:P], Gt_all[:, col:col + 1],
                                   com.ident_f[:, :]))
            t1_m = mm.count

            @sec(b, "vector")
            def _(v):
                v.wait_ge(mm.h, t1_m)
                ev.inc(v.tensor_copy(out=orow[0:1, :], in_=sv["k1"][0:1, 0:P]))
                ctx.aux.release(sv["b1"], ev)
            v1_m = ev.count

            @sec(b, "tensor")
            def _(t):
                t.wait_ge(ev.h, v1_m)
                sv["b2"], sv["k2"] = ctx.aux.acquire(t)
                mm.inc(t.matmul(sv["k2"][:, 0:P], com.ones_row_f[0:1, :],
                                orow[0:1, :], start=True, stop=True))
            t2_m = mm.count

            @sec(b, "vector")
            def _(v, col=col):
                v.wait_ge(mm.h, t2_m)
                v.tensor_single_scalar(out=Dm[:, 0:P], in_=sv["k2"][:, 0:P],
                                       scalar=Gt_all[:, col:col + 1], op=ALU.subtract)
                ctx.aux.release(sv["b2"], ev)
                v.scalar_tensor_tensor(out=Dm[:, P:2 * P], in0=com.m_lo_strict_f[:, :],
                                       scalar=-8192.0, in1=Dm[:, 0:P],
                                       op0=ALU.mult, op1=ALU.add)
                v.tensor_scalar_mul(out=Dm[:, 2 * P:3 * P], in0=Dm[:, 0:P], scalar1=-1.0)
                ev.inc(v.scalar_tensor_tensor(out=Dm[:, 2 * P:3 * P],
                                              in0=com.m_up_strict_f[:, :], scalar=-8192.0,
                                              in1=Dm[:, 2 * P:3 * P],
                                              op0=ALU.mult, op1=ALU.add))
            v2_m = ev.count

            @sec(b, "scalar")
            def _(s):
                s.wait_ge(ev.h, v2_m)
                s.activation(out=Dm[:, P:2 * P], in_=Dm[:, P:2 * P], func=AF.Exp)
                sc.inc(s.activation(out=Dm[:, 2 * P:3 * P], in_=Dm[:, 2 * P:3 * P],
                                    func=AF.Exp))
            s1_m = sc.count

            @sec(b, "tensor")
            def _(t, tc0=tc0):
                t.wait_ge(sc.h, s1_m)
                sv["b3"], sv["k3"] = ctx.aux.acquire(t)
                mm.inc(t.matmul(sv["k3"][:, 0:P], kh_sb[:, tc0:tc0 + P],
                                kh_sb[:, tc0:tc0 + P], start=True, stop=True))
                sv["b4"], sv["k4"] = ctx.aux2.acquire(t)
                mm.inc(t.matmul(sv["k4"][:, 0:P], kh_sb[:, tc0:tc0 + P],
                                qh_sb[:, tc0:tc0 + P], start=True, stop=True))
                sv["b5"], sv["k5"] = ctx.pool.acquire(t)
                k5b = sv["k5"].bitcast(BF16)
                t.matmul(sv["k5"][:, 0:64], kh_sb[:, tc0:tc0 + P], S_bf[:, :],
                         start=True, stop=True)
                t.transpose(k5b[:, 256:320], vh_sb[:, tc0:tc0 + P], com.ident_bf[0:64, 0:64])
                t.transpose(k5b[:, 320:384], gh_sb[:, tc0:tc0 + P], com.ident_bf[0:64, 0:64])
                mm.inc(t.transpose(k5b[:, 384:448], kh_sb[:, tc0:tc0 + P],
                                   com.ident_bf[0:64, 0:64]))
            t3_m = mm.count

            @sec(b, "vector")
            def _(v, col=col):
                v.wait_ge(mm.h, t3_m)
                k5b = sv["k5"].bitcast(BF16)
                v.tensor_tensor(out=Dm[:, 3 * P:4 * P], in0=sv["k3"][:, 0:P],
                                in1=Dm[:, 2 * P:3 * P], op=ALU.mult)
                v.tensor_single_scalar(out=ut[:, 64:128], in_=sv["k5"][:, 0:64],
                                       scalar=Gam_all[:, col:col + 1], op=ALU.mult)
                v.tensor_copy(out=ktm[:, :], in_=k5b[:, 384:448])
                v.tensor_tensor(out=ut[:, 0:64], in0=k5b[:, 256:320],
                                in1=ut[:, 64:128], op=ALU.subtract)
                v.tensor_copy(out=gtm[:, :], in_=k5b[:, 320:384])
                v.tensor_single_scalar(out=ut[:, 0:64], in_=ut[:, 0:64],
                                       scalar=bt_tm[:, col:col + 1], op=ALU.mult)
                v.scalar_tensor_tensor(out=Ap[0][:, :], in0=Dm[:, 3 * P:4 * P],
                                       scalar=bt_tm[:, col:col + 1], in1=com.m_ls_bf[:, :],
                                       op0=ALU.mult, op1=ALU.mult)
                ev.inc(v.tensor_copy(out=vtm[:, :], in_=ut[:, 0:64]))
                ctx.aux.release(sv["b3"], ev)
                ctx.pool.release(sv["b5"], ev)
            v3_m = ev.count

            @sec(b, "tensor")
            def _(t):
                t.wait_ge(ev.h, v3_m)
                sv["b6"], sv["k6"] = ctx.aux.acquire(t)
                mm.inc(t.transpose(sv["k6"].bitcast(BF16)[:, 0:P], Ap[0][:, :],
                                   com.ident_bf[:, :]))
            t4_m = mm.count

            @sec(b, "vector")
            def _(v):
                v.wait_ge(mm.h, t4_m)
                ev.inc(v.tensor_copy(out=Ap[1][:, :], in_=sv["k6"].bitcast(BF16)[:, 0:P]))
                ctx.aux.release(sv["b6"], ev)

            for lvl in range(7):
                su = {}

                @sec(b, "tensor")
                def _(t, lvl=lvl):
                    t.wait_ge(ev.h, ev.count)
                    su["bi"], su["bank"] = ctx.pool.acquire(t)
                    mm.inc(t.matmul(su["bank"][:, 0:64], Ap[1][:, :], vtm[:, :],
                                    start=True, stop=True))
                    if lvl < 6:
                        su["bj"], su["bank2"] = ctx.pool.acquire(t)
                        t.matmul(su["bank2"][:, 0:P], Ap[1][:, :], Ap[0][:, :],
                                 start=True, stop=True)
                        su["bk"], su["bank3"] = ctx.pool.acquire(t)
                        mm.inc(t.matmul(su["bank3"][:, 0:P], Ap[0][:, :], Ap[1][:, :],
                                        start=True, stop=True))
                nl_m = mm.count

                @sec(b, "vector")
                def _(v, lvl=lvl):
                    v.wait_ge(mm.h, nl_m)
                    sgn = -1.0 if lvl == 0 else 1.0
                    v.scalar_tensor_tensor(out=ut[:, 0:64], in0=su["bank"][:, 0:64],
                                           scalar=sgn, in1=ut[:, 0:64],
                                           op0=ALU.mult, op1=ALU.add)
                    if lvl < 6:
                        v.tensor_copy(out=Ap[2][:, :], in_=su["bank2"][:, 0:P])
                        v.tensor_copy(out=Ap[3][:, :], in_=su["bank3"][:, 0:P])
                        ev.inc(v.tensor_copy(out=vtm[:, :], in_=ut[:, 0:64]))
                        ctx.pool.release(su["bi"], ev)
                        ctx.pool.release(su["bj"], ev)
                        ctx.pool.release(su["bk"], ev)
                    else:
                        com.gap(v)
                        ev.inc(v.tensor_copy(out=vtm[:, :], in_=ut[:, 0:64]))
                        ctx.pool.release(su["bi"], ev)
                if lvl < 6:
                    Ap[0], Ap[1], Ap[2], Ap[3] = Ap[2], Ap[3], Ap[0], Ap[1]
            neu_m = ev.count

            @sec(b, "vector")
            def _(v, col=col):
                v.wait_ge(ev.h, neu_m)
                v.tensor_tensor(out=Ap[0][:, :], in0=sv["k4"][:, 0:P],
                                in1=Dm[:, P:2 * P], op=ALU.mult)
                ev.inc(v.tensor_single_scalar(out=wtm[:, :], in_=ut[:, 0:64],
                                              scalar=W_all[:, col:col + 1], op=ALU.mult))
                ctx.aux2.release(sv["b4"], ev)
            v5_m = ev.count

            @sec(b, "tensor")
            def _(t, tc0=tc0):
                t.wait_ge(ev.h, v5_m)
                sv["b8"], sv["k8"] = ctx.pool.acquire(t)
                t.matmul(sv["k8"][:, 0:64], Ap[0][:, :], vtm[:, :], start=True, stop=False)
                mm.inc(t.matmul(sv["k8"][:, 64:128], qh_sb[:, tc0:tc0 + P], S_bf[:, :],
                                start=False, stop=True))
                sv["b9"], sv["k9"] = ctx.pool.acquire(t)
                mm.inc(t.matmul(sv["k9"][0:64, 0:64], ktm[:, :], wtm[:, :],
                                start=True, stop=True))
            t5_m = mm.count

            @sec(b, "vector")
            def _(v, col=col):
                v.wait_ge(mm.h, t5_m)
                v.tensor_single_scalar(out=otm[:, :], in_=sv["k8"][:, 64:128],
                                       scalar=Gam_all[:, col:col + 1], op=ALU.mult)
                v.tensor_single_scalar(out=S_f[:, :], in_=S_f[:, :],
                                       scalar=GamC_all[0:64, col:col + 1], op=ALU.mult)
                v.tensor_tensor(out=otm[:, :], in0=otm[:, :], in1=sv["k8"][:, 0:64],
                                op=ALU.add)
                v.tensor_tensor(out=S_f[:, :], in0=S_f[:, :], in1=sv["k9"][0:64, 0:64],
                                op=ALU.add)
                v.tensor_tensor(out=ut[:, 64:128], in0=otm[:, :], in1=otm[:, :],
                                op=ALU.mult)
                v.tensor_copy(out=S_bf[:, :], in_=S_f[:, :])
                ev.inc(v.reduce_sum(out=msc[:, 0:1], in_=ut[:, 64:128], axis=AX.X))
                ctx.pool.release(sv["b8"], ev)
                ctx.pool.release(sv["b9"], ev)
            v7_m = ev.count

            @sec(b, "scalar")
            def _(s):
                s.wait_ge(ev.h, v7_m)
                s.activation(out=msc[:, 1:2], in_=msc[:, 0:1], func=AF.Ln,
                             bias=com.const_pp(6.4e-4))
                s.activation(out=gtm[:, :], in_=gtm[:, :], func=AF.Silu)
                sc.inc(s.activation(out=msc[:, 1:2], in_=msc[:, 1:2], func=AF.Exp,
                                    scale=-0.5))
            s4_m = sc.count

            @sec(b, "vector")
            def _(v):
                v.wait_ge(sc.h, s4_m)
                v.tensor_single_scalar(out=ut[:, 0:64], in_=gtm[:, :],
                                       scalar=msc[:, 1:2], op=ALU.mult)
                v.tensor_tensor(out=ut[:, 64:128], in0=otm[:, :], in1=nwbc[:, :],
                                op=ALU.mult)
                com.gap(v)
                ev.inc(v.tensor_tensor(out=vtm[:, :], in0=ut[:, 0:64],
                                       in1=ut[:, 64:128], op=ALU.mult))
            v8_m = ev.count

            @sec(b, "tensor")
            def _(t):
                t.wait_ge(ev.h, v8_m)
                sv["b10"], sv["k10"] = ctx.aux.acquire(t)
                mm.inc(t.transpose(sv["k10"].bitcast(BF16)[0:64, 0:P], vtm[:, :],
                                   com.ident_bf[:, :]))
            t6_m = mm.count

            @sec(b, "vector")
            def _(v, tc0=tc0):
                v.wait_ge(mm.h, t6_m)
                ev.inc(v.tensor_copy(out=o_fm_sb[:, tc0:tc0 + P],
                                     in_=sv["k10"].bitcast(BF16)[0:64, 0:P]))
                ctx.aux.release(sv["b10"], ev)

        @sec(b, "sync")
        def _(s, h=h):
            s.wait_ge(ev.h, ev.count)
            qow.inc(s.dma_start(out=odram[h * 64:(h + 1) * 64, :], in_=o_fm_sb[:, :]), 16)
    scan_all = (qow, qow.count)

    oslc = [ctx.bigscr[:, i * NMAX:(i + 1) * NMAX] for i in range(8)]
    qos = b.fresh_qsem()
    for cc in range(ntc):
        t0 = cc * NMAX

        @sec(b, "sync")
        def _(s, cc=cc, t0=t0):
            s.wait_ge(scan_all[0].h, scan_all[1])
            if cc > 0:
                s.wait_ge(mm.h, mm.count)
            for i in range(8):
                qos.inc(s.dma_start(out=oslc[i][:, :],
                                    in_=odram[i * P:(i + 1) * P, t0:t0 + NMAX]), 16)
        os_mark = qos.count
        rc = ResidChunk(b, com, ctx, xdram, cc, pre_waits=[(qos, os_mark)])
        descs = linear_stream_stage(b, com, src_tiles=oslc, wdram_bf=wbf["Wo"],
                                    d_in=1024, d_out=1024, pool=ctx.pool, t0=t0,
                                    tc=NMAX, mm_sem=mm, done_sem=ev, qp=None,
                                    panels=panels, pre_waits=[(qos, os_mark)],
                                    src_t0=0, kw_dma_pre=[wt_pre])

        def epr(e, ot, ps, _t0, rc=rc):
            return e.tensor_tensor(out=rc.tile(ot)[:, :], in0=ps,
                                   in1=rc.tile(ot)[:, :], op=ALU.add)
        run_epilogues(b, "vector", descs, mm, ctx.pool, ev, epr,
                      extra_waits=[rc.dma_mark])
        final = rc.writeback(ev, ev.count)
    return final

from concourse.bass_utils import run_bass_kernel_spmd

T = 4096
NCORE = 8


def _rot_tables():
    dim_exp = 16
    freqs = 2.0 ** np.linspace(0.0, np.log2(10.0 / 2.0), dim_exp)
    t = 1.0 + np.arange(T)
    rad = np.einsum('t,f->tf', t, freqs) * np.pi
    s, c = np.sin(rad), np.cos(rad)
    M = np.ones((64, T), np.float32)
    for j in range(32):
        M[j, :] = s[:, j % 16] + (1.0 if (j % 2 == 1) else -1.0) * c[:, j % 16]
    M2 = np.concatenate([M, M], axis=0).astype(np.float32)
    return (M2 / 8.0).astype(np.float32), M2


def _cols(v):
    return np.ascontiguousarray(np.asarray(v).reshape(-1, 128).T)


def _ct(a):
    return np.ascontiguousarray(np.asarray(a).T)


def _prog_shell(wd_shapes, scratch=(), wbf_shapes=(), ctx_kw=None, outs=()):
    b = B()
    xin = b.dram("xin", [1024, T], F32, kind="ExternalInput")
    xout = b.dram("xout", [1024, T], F32, kind="ExternalOutput")
    wd = {k: b.dram(k, shp, F32, kind="ExternalInput") for k, shp in wd_shapes.items()}
    wbf = {k: b.dram("bf_" + k, shp, BF16, kind="Internal") for k, shp in wbf_shapes}
    scr = {k: b.dram(k, shp, dt, kind="Internal") for k, shp, dt in scratch}
    extra = {k: b.dram(k, shp, F32, kind=kind) for k, shp, kind in outs}
    xdram = b.dram("xres", [1024, T], F32, kind="Internal")
    b.open_block()
    com = Common(b)
    com.emit()
    ctx = SubCtx(b, com, T, **(ctx_kw or {}))
    qi = b.sem("q_in")

    @sec(b, "sync")
    def _(s):
        qi.inc(s.dma_start(out=xdram[:, :], in_=xin[:, :]), 16)

    def hook(final, write_x=True):
        qf = b.sem("q_fin")

        @sec(b, "sync")
        def _(s):
            for s_, c_ in final:
                s.wait_ge(s_.h, c_)
            if write_x:
                qf.inc(s.dma_start(out=xout[:, :], in_=xdram[:, :]), 16)
                s.wait_ge(qf.h, qf.count)
            else:
                s.nop()
        finish(b)
        b.close()
        return b
    return b, com, ctx, wd, wbf, scr, extra, xdram, [(qi, qi.count)], hook


def _build_l0ffn():
    sh = {"U1": [1024, 256], "V1": [256, 4096], "U2": [4096, 256],
          "V2": [256, 1024], "b1c": [128, 32], "b2c": [128, 8]}
    b, com, ctx, wd, wbf, scr, extra, xdram, guards, hook = _prog_shell(sh)
    final = block_l0ffn(b, com, ctx, xdram=xdram, wd=wd, guards=guards)
    return hook(final)


def _build_l0attn():
    names1 = {"wq": [1024, 1024], "wk": [1024, 1024], "wv": [1024, 1024],
              "wo": [1024, 1024], "w1": [1024, 4096], "w2": [4096, 1024],
              "ln1g": [128, 8], "ln1b": [128, 8], "ln2g": [128, 8], "ln2b": [128, 8],
              "bqc": [128, 8], "bkc": [128, 8], "b1c": [128, 32], "b2c": [128, 8],
              "bvrow": [1, 1024]}
    sh = {}
    for L in range(2):
        for k, s in names1.items():
            sh[f"L{L}_{k}"] = s
    sh["mqd"] = [128, T]
    sh["mkd"] = [128, T]
    scratch = [("attres", [1024, T], BF16), ("yres", [1024, T], F32)]
    wbf_shapes = [(f"bf{L}_{k}", names1[k]) for L in range(2)
                  for k in ("wq", "wk", "wv", "wo", "w1", "w2")]
    b, com, ctx, wd, wbf, scr, extra, xdram, guards, hook = _prog_shell(
        sh, scratch=scratch, wbf_shapes=wbf_shapes, ctx_kw={"big_cols": 32 * 512})
    wdl = [{k: wd[f"L{L}_{k}"] for k in names1} for L in range(2)]
    wbfl = [{k: wbf[f"bf{L}_{k}"] for k in ("wq", "wk", "wv", "wo", "w1", "w2")}
            for L in range(2)]
    ydram = scr["yres"]
    ln_fm(b, com, ctx, xdram=xdram, guards=guards, f32_to=ydram)
    qo0, qo1 = b.sem("q_ob0"), b.sem("q_ob1")
    att_guards = block_l0attn(b, com, ctx, xdram=xdram, ydram=ydram,
                              attdram=scr["attres"], wd=wdl, wbf=wbfl,
                              mqd=wd["mqd"], mkd=wd["mkd"],
                              guards=[(qo0, qo0.count), (qo1, qo1.count)])
    big32 = ctx.bigscr[:, :].bitcast(F32)
    ychunk = [big32[:, i * 512:(i + 1) * 512] for i in range(8)]
    final = add_streams(b, com, ctx, xdram=xdram, ydram=ydram, ychunk=ychunk,
                        guards=att_guards)
    return hook(final)


def _build_l1ffn():
    sh = {"w1": [1024, 4096], "w2": [1024, 4096], "h1": [4096, 4096],
          "h2": [4096, 4096], "w3": [4096, 1024],
          "s1c": [128, 32], "s2c": [128, 32], "hs1c": [128, 32], "hs2c": [128, 32],
          "ws3c": [128, 8]}
    wbf_shapes = [(k, sh[k]) for k in ("w1", "w2", "h1", "h2", "w3")]
    b, com, ctx, wd, wbf, scr, extra, xdram, guards, hook = _prog_shell(
        sh, wbf_shapes=wbf_shapes)
    final = block_l1ffn(b, com, ctx, xdram=xdram, wd=wd, wbf=wbf, guards=guards)
    return hook(final)


def _build_l1attn():
    sh = {"fc1T": [1024, 1024], "fc1_bc": [128, 8], "in_projT": [1024, 4096],
          "conv_cw": [128, 64], "conv_cb": [128, 16], "WxB": [2048, 16],
          "WxC": [2048, 16], "Wxdl": [2048, 64], "dtprojT": [64, 2048],
          "dtb_c": [128, 16], "A_log_cols": [128, 256], "D_cols": [128, 16],
          "W_outT": [2048, 1024], "fc2T": [1024, 1024], "fc2_bc": [128, 8]}
    scratch = [("xz", [4096, T], BF16), ("yg", [2048, T], BF16),
               ("ud", [2048, T], BF16)]
    b, com, ctx, wd, wbf, scr, extra, xdram, guards, hook = _prog_shell(
        sh, scratch=scratch, ctx_kw={"big_cols": 32 * 512, "nbuf": 1})
    wbf = {"in_proj": b.dram("bf_inp", [1024, 4096], BF16, kind="Internal"),
           "W_out": b.dram("bf_wout", [2048, 1024], BF16, kind="Internal"),
           "fc1": b.dram("bf_fc1", [1024, 1024], BF16, kind="Internal"),
           "fc2": b.dram("bf_fc2", [1024, 1024], BF16, kind="Internal")}
    final = block_l1attn(b, com, ctx, xdram=xdram, wd=wd, wbf=wbf,
                         xzdram=scr["xz"], ygdram=scr["yg"], udram=scr["ud"],
                         guards=guards)
    return hook(final)


def _build_l2attn():
    sh = {"W_inT": [1024, 4096], "WxB": [2048, 16], "WxC": [2048, 16],
          "Wxd": [2048, 1], "W_outT": [2048, 1024], "conv_cw": [128, 64],
          "conv_cb": [128, 16], "A_log_cols": [128, 256], "D_cols": [128, 16]}
    scratch = [("xz", [4096, T], BF16), ("yg", [2048, T], BF16),
               ("ud", [2048, T], BF16)]
    b, com, ctx, wd, wbf, scr, extra, xdram, guards, hook = _prog_shell(
        sh, scratch=scratch, ctx_kw={"big_cols": 32 * 512})
    wbf = {"W_in": b.dram("bf_W_in", [1024, 4096], BF16, kind="Internal"),
           "W_out": b.dram("bf_W_out", [2048, 1024], BF16, kind="Internal")}
    final = block_l2attn(b, com, ctx, xdram=xdram, wd=wd, wbf=wbf,
                         xzdram=scr["xz"], ygdram=scr["yg"], udram=scr["ud"],
                         guards=guards)
    return hook(final)


def _build_l2ffn(mode):
    sh = {"gwT": [1024, 8], "lb_row": [1, 8], "eg_T": [8192, 512],
          "eu_T": [8192, 512], "ed_T": [4096, 1024], "sg_T": [1024, 1024],
          "su_T": [1024, 1024], "sd_T": [1024, 1024]}
    if mode == "stats":
        outs = [("stats", [1, 16], "ExternalOutput")]
        b, com, ctx, wd, wbf, scr, extra, xdram, guards, hook = _prog_shell(
            sh, outs=outs)
        final = block_l2ffn(b, com, ctx, xdram=xdram, wd=wd, wbf=None,
                            mode="stats", stats_out=extra["stats"], guards=guards)
        return hook(final, write_x=False)
    outs = [("gstats", [8, 16], "ExternalInput")]
    wbf_shapes = [("eg", [8192, 512]), ("eu", [8192, 512]), ("ed", [4096, 1024]),
                  ("sg", [1024, 1024]), ("su", [1024, 1024]), ("sd", [1024, 1024])]
    b, com, ctx, wd, wbf, scr, extra, xdram, guards, hook = _prog_shell(
        sh, wbf_shapes=wbf_shapes, outs=outs)
    final = block_l2ffn(b, com, ctx, xdram=xdram, wd=wd, wbf=wbf, mode="full",
                        gstats=extra["gstats"], guards=guards)
    return hook(final)


def _build_l3ffn(mode):
    sh = {"WrT": [1024, 8], "br_row": [1, 8], "W1_T": [8192, 4096],
          "b1_cols": [128, 256], "W2_T": [32768, 1024], "b2_cols": [128, 64]}
    if mode == "stats":
        outs = [("stats", [8, 1], "ExternalOutput")]
        b, com, ctx, wd, wbf, scr, extra, xdram, guards, hook = _prog_shell(
            sh, ctx_kw={"big_cols": 32 * 512}, outs=outs)
        final = block_l3ffn(b, com, ctx, xdram=xdram, wd=wd, wbf=None,
                            mode="stats", stats_out=extra["stats"], guards=guards)
        return hook(final, write_x=False)
    outs = [("gcnts", [8, 8], "ExternalInput"), ("maskrow", [8, 1], "ExternalInput")]
    wbf_shapes = [("W1", [8192, 4096]), ("W2", [32768, 1024])]
    b, com, ctx, wd, wbf, scr, extra, xdram, guards, hook = _prog_shell(
        sh, ctx_kw={"big_cols": 32 * 512}, wbf_shapes=wbf_shapes, outs=outs)
    final = block_l3ffn(b, com, ctx, xdram=xdram, wd=wd, wbf=wbf, mode="full",
                        gcnts=extra["gcnts"], maskrow=extra["maskrow"],
                        guards=guards)
    return hook(final)


def _build_l3attn():
    sh = {}
    for k in ["WqT", "WkT", "WvT", "WgT", "WoT"]:
        sh[k] = [1024, 1024]
    for k in ["WbT", "WaT"]:
        sh[k] = [1024, 16]
    for k in ["convq_cw", "convk_cw", "convv_cw"]:
        sh[k] = [128, 32]
    sh["alog_col"] = [16, 1]
    sh["dtb_col"] = [16, 1]
    sh["normw_col"] = [64, 1]
    scratch = [("qkvd", [3072, T], BF16), ("gated", [1024, T], BF16),
               ("od", [1024, T], BF16)]
    wbf_shapes = [(k, [1024, 1024]) for k in ("Wq", "Wk", "Wv", "Wg", "Wo")]
    b, com, ctx, wd, wbf, scr, extra, xdram, guards, hook = _prog_shell(
        sh, scratch=scratch, wbf_shapes=wbf_shapes,
        ctx_kw={"big_cols": 3 * T, "nbuf": 1})
    final = block_l3attn(b, com, ctx, xdram=xdram, wd=wd, wbf=wbf,
                         qkvdram=scr["qkvd"], gatedram=scr["gated"],
                         odram=scr["od"], guards=guards)
    return hook(final)


def _prep(inputs):
    g = lambda k: np.asarray(inputs[k])
    w = {}
    w["l0f"] = {"U1": g("L0_ffn_U1"), "V1": g("L0_ffn_V1"), "U2": g("L0_ffn_U2"),
                "V2": g("L0_ffn_V2"), "b1c": _cols(g("L0_ffn_b1")),
                "b2c": _cols(g("L0_ffn_b2"))}
    mq, mk_ = _rot_tables()
    d = {"mqd": mq, "mkd": mk_}
    for L in range(2):
        d[f"L{L}_wq"] = np.ascontiguousarray(g("L0_attn_wq")[L])
        d[f"L{L}_wk"] = np.ascontiguousarray(g("L0_attn_wk")[L])
        d[f"L{L}_wv"] = np.ascontiguousarray(g("L0_attn_wv")[L])
        d[f"L{L}_wo"] = np.ascontiguousarray(g("L0_attn_wo")[L])
        d[f"L{L}_w1"] = np.ascontiguousarray(g("L0_attn_w1")[L])
        d[f"L{L}_w2"] = np.ascontiguousarray(g("L0_attn_w2")[L])
        d[f"L{L}_ln1g"] = _cols(g("L0_attn_ln1_g")[L])
        d[f"L{L}_ln1b"] = _cols(g("L0_attn_ln1_b")[L])
        d[f"L{L}_ln2g"] = _cols(g("L0_attn_ln2_g")[L])
        d[f"L{L}_ln2b"] = _cols(g("L0_attn_ln2_b")[L])
        d[f"L{L}_bqc"] = _cols(g("L0_attn_bq")[L])
        d[f"L{L}_bkc"] = _cols(g("L0_attn_bk")[L])
        d[f"L{L}_b1c"] = _cols(g("L0_attn_b1")[L])
        d[f"L{L}_b2c"] = np.zeros((128, 8), np.float32)
        d[f"L{L}_bvrow"] = np.ascontiguousarray(g("L0_attn_bv")[L].reshape(1, 1024))
    w["l0a"] = d
    w["l1f"] = {"w1": _ct(g("L1_ffn_w1_w")), "w2": _ct(g("L1_ffn_w2_w")),
                "h1": _ct(g("L1_ffn_h1_w")), "h2": _ct(g("L1_ffn_h2_w")),
                "w3": _ct(g("L1_ffn_w3_w")),
                "s1c": _cols(g("L1_ffn_w1_s")), "s2c": _cols(g("L1_ffn_w2_s")),
                "hs1c": _cols(g("L1_ffn_h1_s")), "hs2c": _cols(g("L1_ffn_h2_s")),
                "ws3c": _cols(g("L1_ffn_w3_s"))}
    conv_w = g("L1_attn_conv_w")
    A_log = g("L1_attn_A_log")
    cw_cols = np.zeros((128, 64), np.float32)
    al_cols = np.zeros((128, 256), np.float32)
    for dt_i in range(16):
        for wv_ in range(4):
            cw_cols[:, dt_i * 4 + wv_] = conv_w[dt_i * 128:(dt_i + 1) * 128, wv_]
        al_cols[:, dt_i * 16:(dt_i + 1) * 16] = A_log[dt_i * 128:(dt_i + 1) * 128, :]
    xp = g("L1_attn_x_proj_w")
    w["l1a"] = {"fc1T": _ct(g("L1_attn_fc1_w")), "fc1_bc": _cols(g("L1_attn_fc1_b")),
                "in_projT": _ct(g("L1_attn_in_proj_w")),
                "conv_cw": cw_cols, "conv_cb": _cols(g("L1_attn_conv_b")),
                "WxB": _ct(xp[64:80]), "WxC": _ct(xp[80:96]), "Wxdl": _ct(xp[0:64]),
                "dtprojT": _ct(g("L1_attn_dt_proj_w")),
                "dtb_c": _cols(g("L1_attn_dt_proj_b")),
                "A_log_cols": al_cols, "D_cols": _cols(g("L1_attn_D")),
                "W_outT": _ct(g("L1_attn_out_proj_w")),
                "fc2T": _ct(g("L1_attn_fc2_w")), "fc2_bc": _cols(g("L1_attn_fc2_b"))}
    conv_w2 = g("L2_attn_conv_w").reshape(2048, -1)
    A_log2 = g("L2_attn_A_log")
    cw2 = np.zeros((128, 64), np.float32)
    al2 = np.zeros((128, 256), np.float32)
    for dt_i in range(16):
        for wv_ in range(4):
            cw2[:, dt_i * 4 + wv_] = conv_w2[dt_i * 128:(dt_i + 1) * 128, wv_]
        al2[:, dt_i * 16:(dt_i + 1) * 16] = A_log2[dt_i * 128:(dt_i + 1) * 128, :]
    Wx2 = g("L2_attn_W_x")
    w["l2a"] = {"W_inT": _ct(g("L2_attn_W_in")),
                "WxB": _ct(Wx2[0:16]), "WxC": _ct(Wx2[16:32]), "Wxd": _ct(Wx2[32:33]),
                "W_outT": _ct(g("L2_attn_W_out")),
                "conv_cw": cw2, "conv_cb": _cols(g("L2_attn_conv_b")),
                "A_log_cols": al2, "D_cols": _cols(g("L2_attn_D"))}
    w["l2f"] = {"gwT": _ct(g("L2_ffn_gate_w")),
                "lb_row": np.ascontiguousarray(g("L2_ffn_lb_bias").reshape(1, 8)),
                "eg_T": np.ascontiguousarray(
                    g("L2_ffn_expert_gate_w").transpose(0, 2, 1).reshape(8192, 512)),
                "eu_T": np.ascontiguousarray(
                    g("L2_ffn_expert_up_w").transpose(0, 2, 1).reshape(8192, 512)),
                "ed_T": np.ascontiguousarray(
                    g("L2_ffn_expert_down_w").transpose(0, 2, 1).reshape(4096, 1024)),
                "sg_T": _ct(g("L2_ffn_shared_gate_w")),
                "su_T": _ct(g("L2_ffn_shared_up_w")),
                "sd_T": _ct(g("L2_ffn_shared_down_w"))}
    b1cols = np.zeros((128, 256), np.float32)
    b2cols = np.zeros((128, 64), np.float32)
    b1_ = g("L3_ffn_b1")
    b2_ = g("L3_ffn_b2")
    for e in range(8):
        b1cols[:, e * 32:(e + 1) * 32] = b1_[e].reshape(32, 128).T
        b2cols[:, e * 8:(e + 1) * 8] = b2_[e].reshape(8, 128).T
    w["l3f"] = {"WrT": np.ascontiguousarray(g("L3_ffn_Wr")),
                "br_row": np.ascontiguousarray(g("L3_ffn_br").reshape(1, 8)),
                "W1_T": np.ascontiguousarray(g("L3_ffn_W1").reshape(8192, 4096)),
                "b1_cols": b1cols,
                "W2_T": np.ascontiguousarray(g("L3_ffn_W2").reshape(32768, 1024)),
                "b2_cols": b2cols}
    cwc = lambda cw: np.ascontiguousarray(
        np.transpose(np.asarray(cw).reshape(8, 128, 4), (1, 0, 2)).reshape(128, 32))
    w["l3a"] = {"WqT": _ct(g("L3_attn_Wq")), "WkT": _ct(g("L3_attn_Wk")),
                "WvT": _ct(g("L3_attn_Wv")), "WgT": _ct(g("L3_attn_Wg")),
                "WoT": _ct(g("L3_attn_Wo")),
                "WbT": _ct(g("L3_attn_Wb")), "WaT": _ct(g("L3_attn_Wa")),
                "convq_cw": cwc(g("L3_attn_conv_q")),
                "convk_cw": cwc(g("L3_attn_conv_k")),
                "convv_cw": cwc(g("L3_attn_conv_v")),
                "alog_col": np.ascontiguousarray(g("L3_attn_A_log").reshape(16, 1)),
                "dtb_col": np.ascontiguousarray(g("L3_attn_dt_bias").reshape(16, 1)),
                "normw_col": np.ascontiguousarray(g("L3_attn_norm_w").reshape(64, 1))}
    return w


def _run(b, maps):
    return run_bass_kernel_spmd(b.nc, maps, core_ids=list(range(NCORE))).results


def kernel(**inputs):
    w = _prep(inputs)
    x_in = np.asarray(inputs["L0_attn_x"])
    x8 = [np.ascontiguousarray(x_in[i].T) for i in range(NCORE)]

    def launch(b, extra_per_core=None, shared=None, want=("xout",)):
        nonlocal x8
        maps = []
        for i in range(NCORE):
            m = dict(shared or {})
            m["xin"] = x8[i]
            if extra_per_core is not None:
                m.update(extra_per_core[i])
            maps.append(m)
        res = _run(b, maps)
        outs = [{k: res[i][k] for k in want} for i in range(NCORE)]
        if "xout" in want:
            x8 = [outs[i]["xout"] for i in range(NCORE)]
        return outs

    maskrows = [np.ascontiguousarray(
        (np.arange(8) < i).astype(np.float32).reshape(8, 1)) for i in range(NCORE)]

    launch(_build_l0ffn(), shared=w["l0f"])
    launch(_build_l0attn(), shared=w["l0a"])
    launch(_build_l0ffn(), shared=w["l0f"])
    launch(_build_l1ffn(), shared=w["l1f"])
    launch(_build_l1attn(), shared=w["l1a"])
    launch(_build_l1ffn(), shared=w["l1f"])
    for _rep in range(2):
        st = launch(_build_l2ffn("stats"), shared=w["l2f"], want=("stats",))
        gstats = np.ascontiguousarray(
            np.concatenate([st[i]["stats"] for i in range(NCORE)], 0))
        launch(_build_l2ffn("full"), shared=dict(w["l2f"], gstats=gstats))
        if _rep == 0:
            launch(_build_l2attn(), shared=w["l2a"])
    for _rep in range(2):
        st = launch(_build_l3ffn("stats"), shared=w["l3f"], want=("stats",))
        gcnts = np.ascontiguousarray(
            np.concatenate([st[i]["stats"].reshape(1, 8) for i in range(NCORE)], 0))
        launch(_build_l3ffn("full"), shared=dict(w["l3f"], gcnts=gcnts),
               extra_per_core=[{"maskrow": maskrows[i]} for i in range(NCORE)])
        if _rep == 0:
            launch(_build_l3attn(), shared=w["l3a"])

    out = np.stack([x8[i].T for i in range(NCORE)], 0)
    return np.ascontiguousarray(out.astype(np.float32, copy=False))
